# version 05
# baseline (speedup 1.0000x reference)
from concourse.bass_utils import run_bass_kernel_spmd
import numpy as np
import concourse.bass as bass
import concourse.mybir as mybir

F32 = mybir.dt.float32
BF16 = mybir.dt.bfloat16
AF = mybir.ActivationFunctionType
ALU = mybir.AluOpType

S = 4096
D = 1024
NT = 32
NF = 8
NPAIR = 4
NSLOT = 4
GRP = 3
SPAN = 1536
EPS_LN = 1e-5
EPS_RMS = 1.1920929e-07
NGRP = (NT + GRP - 1) // GRP


class Ctr:
    def __init__(self, h):
        self.h = h
        self.n = 0

    def inc(self, k=1):
        self.n += k
        return self.n


def build(dbg=False):
    nc = bass.Bass("TRN2", target_bir_lowering=False)

    def din(name, shape, dt=F32):
        return nc.dram_tensor(name, shape, dt, kind="ExternalInput").ap()

    x_d = din("x", [S, D])
    xo_d = din("xo", [S, 512])
    wq_d = din("wqT", [D, 512])
    wk_d = din("wkT", [D, 512])
    wv_d = din("wvT", [D, 512])
    wo_d = din("woT", [D, 512])
    bq_d = din("bq", [1, 512])
    bk_d = din("bk", [1, 512])
    bv_d = din("bv", [1, 512])
    bo_d = din("bo", [1, 512])
    g1a_d = din("g1a", [32, 128])
    g2a_d = din("g2a", [4, 16, 16])
    g1b_d = din("g1b", [32, 128])
    g2b_d = din("g2b", [4, 16, 16])
    fb_d = din("fb", [1, 512])
    sgn_d = din("sgn", [128, 1])
    mdiag_d = din("maskdiag", [128, 128])
    mdiagb_d = din("maskdiagb", [128, 128], BF16)
    ident_d = din("ident", [128, 128], BF16)
    eb_d = din("eb", [32, 512])
    et_d = din("et", [16, 512])
    rall_d = din("rall", [32, 512])
    tall_d = din("tall", [16, 512])

    out_d = nc.dram_tensor("out", [S, 512], F32, kind="ExternalOutput").ap()

    ntm_d = nc.dram_tensor("ntm_scr", [S, D], BF16)
    cc_in = nc.dram_tensor("cc_in", [512, S], BF16)
    cc_out = nc.dram_tensor("cc_out", [1024, S], BF16)
    ntmT_d = nc.dram_tensor("ntmT_scr", [1024, S], BF16)

    dz_d = dn_d = None
    if dbg:
        dz_d = nc.dram_tensor("d_z", [1024, S], BF16, kind="ExternalOutput").ap()
        dn_d = nc.dram_tensor("d_ntm", [S, D], BF16, kind="ExternalOutput").ap()
        dnt_d = nc.dram_tensor("d_ntmT", [1024, S], BF16,
                               kind="ExternalOutput").ap()
        dwf_d = nc.dram_tensor("d_wf", [128, 4096], BF16,
                               kind="ExternalOutput").ap()

    OPS = []

    def add(eng, fn, waits=(), inc=None):
        OPS.append((eng, [(h, v) for (h, v) in waits if v > 0], fn, inc))

    import contextlib
    with contextlib.ExitStack() as ES:
        s_inv = ES.enter_context(nc.sbuf_tensor("s_inv", [128, NT], F32))
        lnt = ES.enter_context(nc.sbuf_tensor("lnt", [128, 32], F32))
        rows = ES.enter_context(nc.sbuf_tensor("rows", [1, 1536], F32))
        fbrow = ES.enter_context(nc.sbuf_tensor("fbrow", [1, 512], F32))
        ones_r = ES.enter_context(nc.sbuf_tensor("ones_r", [1, 512], F32))
        sgn_s = ES.enter_context(nc.sbuf_tensor("sgn_s", [128, 1], F32))
        epsln = ES.enter_context(nc.sbuf_tensor("epsln", [128, 1], F32))
        epsrm = ES.enter_context(nc.sbuf_tensor("epsrm", [128, 1], F32))
        e1e2 = ES.enter_context(nc.sbuf_tensor("e1e2", [128, 1], F32))
        dumc = ES.enter_context(nc.sbuf_tensor("dumc", [128, 1], F32))
        SA = ES.enter_context(nc.psum_tensor("SA", [128, 1024], F32))
        SB = ES.enter_context(nc.psum_tensor("SB", [128, 1024], F32))
        SC = ES.enter_context(nc.psum_tensor("SC", [128, 1024], F32))
        pool = ES.enter_context(nc.psum_tensor("pool", [128, 1024], F32))
        _m_x = ES.enter_context(nc.semaphore("m_x"))
        _m_xb = ES.enter_context(nc.semaphore("m_xb"))
        _m_w = ES.enter_context(nc.semaphore("m_w"))
        _m_nst = ES.enter_context(nc.semaphore("m_nst"))
        _m_tr = ES.enter_context(nc.semaphore("m_tr"))
        _m_zst = ES.enter_context(nc.semaphore("m_zst"))
        _m_cl = ES.enter_context(nc.semaphore("m_cl"))
        _m_out = ES.enter_context(nc.semaphore("m_out"))
        _m_mm = ES.enter_context(nc.semaphore("m_mm"))
        _m_sc = ES.enter_context(nc.semaphore("m_sc"))
        _m_ve = ES.enter_context(nc.semaphore("m_ve"))
        _m_cc = ES.enter_context(nc.semaphore("m_cc"))
        _m_hz = ES.enter_context(nc.semaphore("m_hz"))
        _m_w0 = ES.enter_context(nc.semaphore("m_w0"))
        _m_zb0 = ES.enter_context(nc.semaphore("m_zb0"))
        _m_zb1 = ES.enter_context(nc.semaphore("m_zb1"))
        _m_nt0 = ES.enter_context(nc.semaphore("m_nt0"))
        _m_nt1 = ES.enter_context(nc.semaphore("m_nt1"))
        _m_xo0 = ES.enter_context(nc.semaphore("m_xo0"))
        _m_xo1 = ES.enter_context(nc.semaphore("m_xo1"))
        _m_o0 = ES.enter_context(nc.semaphore("m_o0"))
        _m_o1 = ES.enter_context(nc.semaphore("m_o1"))
        _m_ns0 = ES.enter_context(nc.semaphore("m_ns0"))
        _m_ns1 = ES.enter_context(nc.semaphore("m_ns1"))
        block = ES.enter_context(nc.Block())
        ESQ = ES.enter_context(contextlib.ExitStack())
        Vt = ESQ.enter_context(nc.sbuf_tensor("V_tm", [128, NT * 512], BF16))
        Qa = ESQ.enter_context(nc.sbuf_tensor("Q_all", [128, NPAIR * S], BF16))
        Ka = ESQ.enter_context(nc.sbuf_tensor("K_all", [128, NPAIR * S], BF16))
        ES1 = ES.enter_context(contextlib.ExitStack())
        nT = ES1.enter_context(nc.sbuf_tensor("normedT", [128, NF * S], BF16))
        wq_s = ES1.enter_context(nc.sbuf_tensor("wq_s", [128, NF * 512], BF16))
        wk_s = ES1.enter_context(nc.sbuf_tensor("wk_s", [128, NF * 512], BF16))
        wv_s = ES1.enter_context(nc.sbuf_tensor("wv_s", [128, NF * 512], BF16))
        xin = ES1.enter_context(nc.sbuf_tensor("xin", [128, 2 * D], F32))
        nrm = ES1.enter_context(nc.sbuf_tensor("nrm", [128, 2 * D], BF16))
        c_x, c_xb = Ctr(_m_x), Ctr(_m_xb)
        c_w, c_nst, c_tr = Ctr(_m_w), Ctr(_m_nst), Ctr(_m_tr)
        c_zst, c_cl, c_out = Ctr(_m_zst), Ctr(_m_cl), Ctr(_m_out)
        c_mm, c_sc, c_ve, c_cc = Ctr(_m_mm), Ctr(_m_sc), Ctr(_m_ve), Ctr(_m_cc)
        c_hz = Ctr(_m_hz)

        c_w0 = Ctr(_m_w0)
        c_zb = [Ctr(_m_zb0), Ctr(_m_zb1)]
        c_nt = [Ctr(_m_nt0), Ctr(_m_nt1)]
        c_xo = [Ctr(_m_xo0), Ctr(_m_xo1)]
        c_o = [Ctr(_m_o0), Ctr(_m_o1)]
        c_ns = [Ctr(_m_ns0), Ctr(_m_ns1)]

        def hz(v, inst):
            inst.then_inc(c_hz.h, 1)
            v.wait_ge(c_hz.h, c_hz.inc())

        def ph0(g):
            g.memset(ones_r[:, :], 1.0)
            g.memset(epsln[:, :], EPS_LN)
            g.memset(epsrm[:, :], EPS_RMS)
            g.memset(e1e2[:, :], 1e-5 * 1.1920929e-07)
            g.memset(dumc[:, :], 0.0)
        add("gpsimd", ph0)

        def gload(dst_ap, src_ap):
            add("gpsimd",
                (lambda g, d=dst_ap, s=src_ap: g.dma_start(d, s)),
                inc=(c_w0.h, 16))
            return c_w0.inc(16)

        gload(sgn_s[:, :], sgn_d)
        gload(rows[0:1, 0:512], bq_d)
        gload(rows[0:1, 512:1024], bk_d)
        gload(rows[0:1, 1024:1536], bv_d)
        gload(fbrow[0:1, :], fb_d)
        for wdst, wsrc in ((wq_s, wq_d), (wk_s, wk_d), (wv_s, wv_d)):
            gload(wdst[:, :].rearrange("p (fc j) -> p fc j", fc=NF),
                  wsrc.rearrange("(fc p) j -> p fc j", p=128))
        W_QKV = c_w0.n
        W_ALL = c_w0.n

        ln_done = [0] * NT
        nst_done = [0] * NT
        E2 = 1.1920929e-07
        ln_sc1 = {}
        def ln_stage_a(i):
            buf = (i % 2) * D
            L = (i % 2) * 16
            eng = "sync" if i % 2 == 0 else "scalar"
            cc = c_x if i % 2 == 0 else c_xb
            add(eng,
                (lambda e, i=i, buf=buf: e.dma_start(
                    xin[:, buf:buf + D], x_d[i * 128:(i + 1) * 128, :])),
                waits=[(c_sc.h, ln_done[i - 2] if i >= 2 else 0)],
                inc=(cc.h, 16))
            xl = cc.inc(16)

            def fn(v, buf=buf, L=L):
                v.tensor_scalar(nrm[:, buf:buf + D], xin[:, buf:buf + D], 1.0,
                                0.0, ALU.mult, ALU.add, accum_out=lnt[:, L + 0:L + 1])
                v.scalar_tensor_tensor(nrm[:, buf:buf + D], xin[:, buf:buf + D],
                                       1.0, xin[:, buf:buf + D], ALU.mult,
                                       ALU.mult, accum_out=lnt[:, L + 1:L + 2])
                v.tensor_scalar_mul(lnt[:, L + 2:L + 3], lnt[:, L + 0:L + 1], 1.0 / D)
                v.tensor_scalar_mul(lnt[:, L + 3:L + 4], lnt[:, L + 1:L + 2], 1.0 / D)
                v.tensor_mul(lnt[:, L + 4:L + 5], lnt[:, L + 2:L + 3], lnt[:, L + 2:L + 3])
                v.tensor_copy(lnt[:, L + 15:L + 16], lnt[:, L + 0:L + 1])
                v.scalar_tensor_tensor(lnt[:, L + 5:L + 6], lnt[:, L + 3:L + 4], 1.0,
                                       lnt[:, L + 4:L + 5], ALU.mult, ALU.subtract)
                v.tensor_copy(lnt[:, L + 15:L + 16], lnt[:, L + 0:L + 1])
                return v.scalar_tensor_tensor(lnt[:, L + 6:L + 7], lnt[:, L + 5:L + 6],
                                              1.0 + E2, e1e2[:, :], ALU.mult,
                                              ALU.add)
            add("vector", fn,
                waits=([(cc.h, xl)] +
                       ([(c_ns[nst_done[i - 2][0]].h, nst_done[i - 2][1])]
                        if i >= 2 else [])),
                inc=(c_ve.h, 1))
            ve1 = c_ve.inc()

            def fn(s, L=L):
                s.activation(lnt[:, L + 7:L + 8], lnt[:, L + 6:L + 7], AF.Sqrt)
                return s.activation(lnt[:, L + 8:L + 9], lnt[:, L + 5:L + 6], AF.Sqrt,
                                    bias=epsln[:, :])
            add("scalar", fn, waits=[(c_ve.h, ve1), (c_w0.h, 16)],
                inc=(c_sc.h, 1))
            ln_sc1[i] = c_sc.inc()


        def ln_stage_b(i):
            buf = (i % 2) * D
            L = (i % 2) * 16
            cc = c_x if i % 2 == 0 else c_xb
            def fn(v, i=i, L=L):
                v.reciprocal(lnt[:, L + 9:L + 10], lnt[:, L + 7:L + 8])
                v.reciprocal(lnt[:, L + 10:L + 11], lnt[:, L + 8:L + 9])
                v.tensor_copy(lnt[:, L + 15:L + 16], lnt[:, L + 0:L + 1])
                v.tensor_mul(s_inv[:, i:i + 1], lnt[:, L + 7:L + 8], lnt[:, L + 10:L + 11])
                return v.scalar_tensor_tensor(lnt[:, L + 11:L + 12], lnt[:, L + 2:L + 3], -1.0,
                                              lnt[:, L + 9:L + 10], ALU.mult, ALU.mult)
            add("vector", fn, waits=[(c_sc.h, ln_sc1[i])], inc=(c_ve.h, 1))
            ve2 = c_ve.inc()

            add("scalar",
                (lambda s, i=i, buf=buf, L=L: s.activation(
                    nrm[:, buf:buf + D], xin[:, buf:buf + D], AF.Identity,
                    bias=lnt[:, L + 11:L + 12], scale=lnt[:, L + 9:L + 10])),
                waits=[(c_ve.h, ve2)], inc=(c_sc.h, 1))
            ln_done[i] = c_sc.inc()

            cns = c_ns[i % 2]
            add("gpsimd",
                (lambda g, i=i, buf=buf: g.dma_start(
                    ntm_d[i * 128:(i + 1) * 128, :], nrm[:, buf:buf + D])),
                waits=[(c_sc.h, ln_done[i])], inc=(cns.h, 16))
            nst_done[i] = (i % 2, cns.inc(16))

        for i in range(NT):
            ln_stage_a(i)
            if i >= 1:
                ln_stage_b(i - 1)
        ln_stage_b(NT - 1)
        NST_ALL = (c_ns[0].n, c_ns[1].n)

        for fc in range(NF):
            add("sync",
                (lambda e, fc=fc: e.dma_start(
                    nT[:, fc * S:(fc + 1) * S],
                    ntm_d[:, fc * 128:(fc + 1) * 128], transpose=True)),
                waits=[(c_ns[0].h, NST_ALL[0]), (c_ns[1].h, NST_ALL[1]),
                       (c_tr.h, c_tr.n)],
                inc=(c_tr.h, 16))
            c_tr.inc(16)
            add("gpsimd",
                (lambda g, fc=fc: g.dma_start(
                    ntmT_d[fc * 128:(fc + 1) * 128, :],
                    nT[:, fc * S:(fc + 1) * S])),
                waits=[(c_tr.h, c_tr.n)], inc=(c_w.h, 16))
            c_w.inc(16)
        TR_ALL = c_tr.n
        NTMT_ALL = c_w.n

        pool_free = [0, 0]
        pslot = [0]
        qk_done = [0] * NPAIR

        def build_fm(dst, dcol, tcol, w_s, brow, pcols):
            half = pslot[0] % 2
            pslot[0] += 1
            ps = pool[:, half * 512:half * 512 + 512]

            def fn(t, ps=ps, w_s=w_s, brow=brow, pcols=pcols, tcol=tcol):
                for fc in range(NF):
                    t.matmul(ps, w_s[:, fc * 512 + pcols:fc * 512 + pcols + 128],
                             nT[:, fc * S + tcol:fc * S + tcol + 512],
                             start=(fc == 0), stop=False)
                return t.matmul(ps, brow, ones_r[0:1, 0:512], start=False,
                                stop=True)
            add("tensor", fn,
                waits=[(c_tr.h, TR_ALL), (c_w0.h, W_QKV),
                       (c_ve.h, pool_free[half])],
                inc=(c_mm.h, 1))
            mmv = c_mm.inc()
            add("vector",
                (lambda v, dst=dst, dcol=dcol, ps=ps: v.tensor_copy(
                    dst[:, dcol:dcol + 512], ps)),
                waits=[(c_mm.h, mmv)], inc=(c_ve.h, 1))
            pool_free[half] = c_ve.inc()

        for p in range(NPAIR):
            for tc in range(8):
                build_fm(Qa, p * S + tc * 512, tc * 512, wq_s,
                         rows[0:1, p * 128:p * 128 + 128], p * 128)
                build_fm(Ka, p * S + tc * 512, tc * 512, wk_s,
                         rows[0:1, 512 + p * 128:512 + p * 128 + 128], p * 128)
            qk_done[p] = c_ve.n

        for tt in range(NT):
            half = pslot[0] % 2
            pslot[0] += 1
            ps = pool[:, half * 512:half * 512 + 512]

            def fn(t, tt=tt, ps=ps):
                for fc in range(NF):
                    t.matmul(ps, nT[:, fc * S + tt * 128:fc * S + tt * 128 + 128],
                             wv_s[:, fc * 512:(fc + 1) * 512],
                             start=(fc == 0), stop=False)
                return t.matmul(ps, ones_r[0:1, 0:128], rows[0:1, 1024:1536],
                                start=False, stop=True)
            add("tensor", fn,
                waits=[(c_tr.h, TR_ALL), (c_w0.h, W_QKV),
                       (c_ve.h, pool_free[half])],
                inc=(c_mm.h, 1))
            mmv = c_mm.inc()
            add("vector",
                (lambda v, tt=tt, ps=ps: v.tensor_copy(
                    Vt[:, tt * 512:(tt + 1) * 512], ps)),
                waits=[(c_mm.h, mmv)], inc=(c_ve.h, 1))
            pool_free[half] = c_ve.inc()

        ES1.close()
        PH3_VE = c_ve.n
        ES2 = ES.enter_context(contextlib.ExitStack())
        Ab = ES2.enter_context(nc.sbuf_tensor("A_buf", [128, NSLOT * 2 * S],
                                              BF16))
        z_sb = ES2.enter_context(nc.sbuf_tensor("z_sb", [128, S], F32))
        z_bf = ES2.enter_context(nc.sbuf_tensor("z_bf", [128, S], BF16))
        mdiag = ES2.enter_context(nc.sbuf_tensor("mdiag", [128, 128], F32))
        mdiagb = ES2.enter_context(nc.sbuf_tensor("mdiagb_s", [128, 128], BF16))
        ident = ES2.enter_context(nc.sbuf_tensor("ident_s", [128, 128], BF16))
        inv9 = ES2.enter_context(nc.sbuf_tensor("inv9", [128, 1], F32))
        bscr = ES2.enter_context(nc.sbuf_tensor("bscr", [128, 512], BF16))
        zacc = ES2.enter_context(nc.sbuf_tensor("zacc", [128, 2], F32))
        zpart = ES2.enter_context(nc.sbuf_tensor("zpart", [128, 16], F32))
        mxb = ES2.enter_context(nc.sbuf_tensor("mxb", [128, 32], F32))
        negC = ES2.enter_context(nc.sbuf_tensor("negC", [128, 6], F32))
        rcp = ES2.enter_context(nc.sbuf_tensor("rcp", [128, 2], F32))
        Vp = ES2.enter_context(nc.sbuf_tensor("Vp", [128, NSLOT * 128], BF16))
        add("gpsimd", (lambda g: None), waits=[(c_w.h, NTMT_ALL)])
        add("gpsimd",
            (lambda g: g.dma_start(mdiag[:, :], mdiag_d)),
            waits=[(c_ve.h, PH3_VE)], inc=(c_w.h, 16))
        c_w.inc(16)
        add("gpsimd", (lambda g: g.dma_start(mdiagb[:, :], mdiagb_d)),
            inc=(c_w.h, 16))
        c_w.inc(16)
        add("gpsimd", (lambda g: g.dma_start(ident[:, :], ident_d)),
            inc=(c_w.h, 16))
        c_w.inc(16)
        add("gpsimd", (lambda g: g.memset(inv9[:, :], 1.0 / 9.0)),
            inc=(c_w.h, 1))
        c_w.inc(1)
        MD_LD = c_w.n
        slot_free = [0] * NSLOT
        SA_free = [0]
        SB_free = [0]
        SC_free = [0, 0]
        SCb_free = [0, 0]
        ln_used = [0]
        zpool_free = [0, 0]
        vp_ev_of = {}
        C_done = {}
        p2_exp_of = {}
        p2_meta = {}
        zst_prev = [0]

        def pieces_of(lo, hi, step):
            out = []
            a = lo
            while a < hi:
                out.append((a, min(a + step, hi)))
                a = min(a + step, hi)
            return out

        def emit_strip_p1(p, st):
            gst = p * NT + st
            g = st // GRP
            k0 = st * 128
            gq0 = g * GRP * 128
            loc0 = k0 - gq0
            egrid = S - gq0
            par = st % 2
            beta = (st % 5 == 4)
            pcs = pieces_of(loc0, egrid, 512)
            rnd = 0
            for pi, (a, b) in enumerate(pcs):
                w = b - a
                for h in range(2):
                    lo, hi = (0, 64) if h == 0 else (64, 128)
                    tp = (0, 0) if h == 0 else (64, 0)
                    half = rnd % 2
                    rnd += 1

                    def fn(t, a=a, w=w, lo=lo, hi=hi, tp=tp, p=p, k0=k0,
                           gq0=gq0, half=half, isdiag=(pi == 0)):
                        qg = gq0 + a
                        sc = SC[:, half * 512:half * 512 + w]
                        mmi = t.matmul(sc,
                                       Ka[lo:hi, p * S + k0:p * S + k0 + 128],
                                       Qa[lo:hi, p * S + qg:p * S + qg + w],
                                       start=True, stop=not isdiag,
                                       tile_position=tp)
                        if isdiag:
                            mmi = t.matmul(SC[:, half * 512:half * 512 + 128],
                                           ident[:, :], mdiagb[:, :],
                                           start=False, stop=True)
                        return mmi
                    add("tensor", fn,
                        waits=[(c_ve.h, max(SC_free[half], qk_done[p])),
                               (c_sc.h, SCb_free[half]),
                               (c_w.h, MD_LD)],
                        inc=(c_mm.h, 1))
                    mm = c_mm.inc()

                    mcol = par * 16 + h * 8 + pi
                    if not beta:
                        def fn(v, mm=mm, w=w, mcol=mcol, half=half):
                            return v.reduce_max(
                                mxb[:, mcol:mcol + 1],
                                SC[:, half * 512:half * 512 + w],
                                axis=mybir.AxisListType.X)
                        add("vector", fn, waits=[(c_mm.h, mm)],
                            inc=(c_ve.h, 1))
                        SC_free[half] = c_ve.inc()
                    else:
                        def fn(s, mm=mm, w=w, mcol=mcol, half=half):
                            return s.activation(
                                bscr[:, 0:w],
                                SC[:, half * 512:half * 512 + w], AF.Exp,
                                scale=inv9[:, :],
                                accum_out=mxb[:, mcol:mcol + 1])
                        add("scalar", fn, waits=[(c_mm.h, mm)],
                            inc=(c_sc.h, 1))
                        SCb_free[half] = c_sc.inc()
            npc = len(pcs)

            if not beta:
                def fn(v, par=par, npc=npc):
                    last = None
                    for h in range(2):
                        base = par * 16 + h * 8
                        last = v.reduce_max(
                            negC[:, par * 2 + h:par * 2 + h + 1],
                            mxb[:, base:base + npc],
                            axis=mybir.AxisListType.X, negate=True)
                    return last
                add("vector", fn,
                    waits=[(c_sc.h, p2_exp_of.get(gst - 2, 0))],
                    inc=(c_ve.h, 1))
                C_done[gst] = c_ve.inc()
            else:
                def fn(v, par=par, npc=npc):
                    ins = None
                    for h in range(2):
                        base = par * 16 + h * 8
                        ins = v.tensor_scalar(bscr[:, 0:npc],
                                              mxb[:, base:base + npc],
                                              1.0, 0.0, ALU.mult, ALU.add,
                                              accum_out=negC[:, 4 + h:5 + h])
                    return ins
                add("vector", fn,
                    waits=[(c_sc.h, p2_exp_of.get(gst - 2, 0)),
                           (c_sc.h, SCb_free[0]), (c_sc.h, SCb_free[1]),
                           (c_sc.h, ln_used[0])],
                    inc=(c_ve.h, 1))
                zve = c_ve.inc()

                def fn(s):
                    s.activation(negC[:, 4:5], negC[:, 4:5], AF.Ln)
                    return s.activation(negC[:, 5:6], negC[:, 5:6], AF.Ln)
                add("scalar", fn, waits=[(c_ve.h, zve)], inc=(c_sc.h, 1))
                lnv = c_sc.inc()
                ln_used[0] = lnv

                def fn(v, par=par):
                    v.tensor_scalar_mul(negC[:, par * 2:par * 2 + 1],
                                        negC[:, 4:5], -9.0)
                    return v.tensor_scalar_mul(
                        negC[:, par * 2 + 1:par * 2 + 2], negC[:, 5:6], -9.0)
                add("vector", fn, waits=[(c_sc.h, lnv)], inc=(c_ve.h, 1))
                C_done[gst] = c_ve.inc()

        def emit_strip_p2(p, st):
            gst = p * NT + st
            g = st // GRP
            slot = st % NSLOT
            k0 = st * 128
            gq0 = g * GRP * 128
            loc0 = k0 - gq0
            egrid = S - gq0
            par = st % 2
            acolA = slot * 2 * S
            acolB = slot * 2 * S + S
            spans = pieces_of(loc0, egrid, 1024)
            zc0 = par * 8
            nspan = len(spans)
            prev2 = vp_ev_of.get(gst - 2, 0)
            for spi, (a, b) in enumerate(spans):
                w = b - a
                isfirst = (spi == 0)
                for h in range(2):
                    Sps = SA if h == 0 else SB
                    lo, hi = (0, 64) if h == 0 else (64, 128)
                    tp = (0, 0) if h == 0 else (64, 0)
                    sfree = SA_free if h == 0 else SB_free

                    def fn(t, Sps=Sps, lo=lo, hi=hi, tp=tp, a=a, w=w, p=p,
                           k0=k0, gq0=gq0, isfirst=isfirst):
                        c0 = 0
                        last = None
                        while c0 < w:
                            c1 = min(c0 + 512, w)
                            qg = gq0 + a + c0
                            diag = (isfirst and c0 == 0)
                            last = t.matmul(
                                Sps[:, c0:c1],
                                Ka[lo:hi, p * S + k0:p * S + k0 + 128],
                                Qa[lo:hi, p * S + qg:p * S + qg + (c1 - c0)],
                                start=True, stop=not diag, tile_position=tp)
                            if diag:
                                last = t.matmul(Sps[:, 0:128], ident[:, :],
                                                mdiagb[:, :], start=False,
                                                stop=True)
                            c0 = c1
                        return last
                    add("tensor", fn, waits=[(c_sc.h, sfree[0])],
                        inc=(c_mm.h, 1))
                    mm = c_mm.inc()

                    acol = acolA if h == 0 else acolB
                    zcol = zc0 + (4 * h) + spi

                    def fn(s, Sps=Sps, w=w, a=a, acol=acol, zcol=zcol, h=h,
                           par=par):
                        return s.activation(
                            Ab[:, acol + a:acol + a + w], Sps[:, 0:w], AF.Exp,
                            bias=negC[:, par * 2 + h:par * 2 + h + 1],
                            accum_out=zpart[:, zcol:zcol + 1])
                    add("scalar", fn,
                        waits=[(c_mm.h, mm), (c_ve.h, prev2),
                               (c_ve.h, C_done[gst])],
                        inc=(c_sc.h, 1))
                    sfree[0] = c_sc.inc()

            p2_exp_of[gst] = max(SA_free[0], SB_free[0])
            p2_meta[gst] = (slot, zc0, nspan)

        def emit_vprime(p, st):
            gst = p * NT + st
            slot, zc0, nspan = p2_meta[gst]

            def fn(v, p=p, st=st, slot=slot, zc0=zc0, nspan=nspan):
                if nspan == 1:
                    v.reciprocal(rcp[:, 0:1], zpart[:, zc0:zc0 + 1])
                    v.reciprocal(rcp[:, 1:2], zpart[:, zc0 + 4:zc0 + 5])
                else:
                    for h in range(2):
                        b = zc0 + 4 * h
                        v.tensor_add(zacc[:, h:h + 1], zpart[:, b:b + 1],
                                     zpart[:, b + 1:b + 2])
                    for k in range(2, nspan):
                        for h in range(2):
                            b = zc0 + 4 * h
                            v.tensor_add(zacc[:, h:h + 1], zacc[:, h:h + 1],
                                         zpart[:, b + k:b + k + 1])
                    v.reciprocal(rcp[:, 0:1], zacc[:, 0:1])
                    v.reciprocal(rcp[:, 1:2], zacc[:, 1:2])
                last = None
                for h in range(2):
                    last = v.tensor_scalar_mul(
                        Vp[:, slot * 128 + h * 64:slot * 128 + h * 64 + 64],
                        Vt[:, st * 512 + (2 * p + h) * 64:
                           st * 512 + (2 * p + h) * 64 + 64],
                        rcp[:, h:h + 1])
                return last
            add("vector", fn,
                waits=[(c_sc.h, p2_exp_of[gst]), (c_mm.h, slot_free[slot])],
                inc=(c_ve.h, 1))
            vp_ev_of[gst] = c_ve.inc()

        def emit_av(p, g):
            strips = list(range(g * GRP, min((g + 1) * GRP, NT)))
            gq0 = g * GRP * 128
            egrid = S - gq0
            nchunk = (egrid + 511) // 512
            vpe = max(vp_ev_of[p * NT + st] for st in strips)
            for j in range(nchunk):
                q0 = j * 512
                w = min(512, egrid - q0)
                half = j % 2

                def fn(t, strips=strips, gq0=gq0, j=j, q0=q0, w=w, half=half):
                    live = [st for st in strips
                            if (j > 0 or (st * 128 - gq0) < q0 + w)]
                    last = None
                    for idx, st in enumerate(live):
                        slot = st % NSLOT
                        loc0 = st * 128 - gq0
                        cs = max(0, loc0 - q0) if j == 0 else 0
                        first = (idx == 0)
                        lastf = (idx == len(live) - 1)
                        t.matmul(
                            pool[0:64, half * 512 + cs:half * 512 + w],
                            Vp[:, slot * 128:slot * 128 + 64],
                            Ab[:, slot * 2 * S + q0 + cs:
                               slot * 2 * S + q0 + w],
                            start=first, stop=lastf, tile_position=(0, 0))
                        last = t.matmul(
                            pool[64:128, half * 512 + cs:half * 512 + w],
                            Vp[:, slot * 128 + 64:slot * 128 + 128],
                            Ab[:, slot * 2 * S + S + q0 + cs:
                               slot * 2 * S + S + q0 + w],
                            start=first, stop=lastf, tile_position=(0, 64))
                    return last
                add("tensor", fn,
                    waits=[(c_ve.h, max(vpe, zpool_free[half]))],
                    inc=(c_mm.h, 1))
                mmv = c_mm.inc()
                for st in strips:
                    slot_free[st % NSLOT] = mmv

                def fn(v, g=g, gq0=gq0, q0=q0, w=w, half=half):
                    zp = pool[:, half * 512:half * 512 + w]
                    dst = z_sb[:, gq0 + q0:gq0 + q0 + w]
                    if g == 0:
                        return v.tensor_copy(dst, zp)
                    return v.tensor_add(dst, dst, zp)
                add("vector", fn, waits=[(c_mm.h, mmv)], inc=(c_ve.h, 1))
                zpool_free[half] = c_ve.inc()

        for p in range(NPAIR):
            emit_strip_p1(p, 0)
            emit_strip_p1(p, 1)
            for st in range(NT):
                emit_strip_p2(p, st)
                if st + 2 < NT:
                    emit_strip_p1(p, st + 2)
                emit_vprime(p, st)
                if st % GRP == GRP - 1 or st == NT - 1:
                    emit_av(p, st // GRP)

            def fn(g):
                return g.tensor_copy(z_bf[:, :], z_sb[:, :])
            add("gpsimd", fn,
                waits=[(c_zst.h, zst_prev[0]), (c_ve.h, zpool_free[0]),
                       (c_ve.h, zpool_free[1])],
                inc=(c_w.h, 1))
            zcast = c_w.inc()

            add("sync",
                (lambda e, p=p: e.dma_start(cc_in[p * 128:(p + 1) * 128, :],
                                            z_bf[:, :])),
                waits=[(c_w.h, zcast)], inc=(c_zst.h, 16))
            zst_prev[0] = c_zst.inc(16)

            def fn(g, p=p):
                return g.collective_compute(
                    "AllGather", ALU.bypass,
                    replica_groups=[[0, 1], [2, 3], [4, 5], [6, 7]],
                    ins=[cc_in[p * 128:(p + 1) * 128, :].opt()],
                    outs=[cc_out[p * 256:(p + 1) * 256, :].opt()],
                )
            add("gpsimd", fn, waits=[(c_zst.h, zst_prev[0])], inc=(c_cc.h, 1))
            c_cc.inc()

        ZST_ALL = c_zst.n
        EXP_ALL = c_sc.n
        CC_ALL = c_cc.n

        ES2.close()
        ESQ.close()
        ES3 = ES.enter_context(contextlib.ExitStack())
        wo_s = ES3.enter_context(nc.sbuf_tensor("wo_s", [128, NF * 512], BF16))
        g1s = ES3.enter_context(nc.sbuf_tensor("g1s", [32, 256], F32))
        g2s = ES3.enter_context(nc.sbuf_tensor("g2s", [16, 128], F32))
        ebs = ES3.enter_context(nc.sbuf_tensor("ebs", [32, 512], F32))
        ets = ES3.enter_context(nc.sbuf_tensor("ets", [16, 512], F32))
        ralls = ES3.enter_context(nc.sbuf_tensor("ralls", [32, 512], F32))
        talls = ES3.enter_context(nc.sbuf_tensor("talls", [16, 512], F32))
        ucs = ES3.enter_context(nc.sbuf_tensor("ucs", [32, 512], F32))
        vcs = ES3.enter_context(nc.sbuf_tensor("vcs", [16, 512], F32))
        wtmp = ES3.enter_context(nc.sbuf_tensor("wtmp", [128, 512], F32))
        wtmp2 = ES3.enter_context(nc.sbuf_tensor("wtmp2", [128, 512], F32))
        wacc = ES3.enter_context(nc.sbuf_tensor("wacc", [128, 2048], F32))
        wfa = ES3.enter_context(nc.sbuf_tensor("wfa", [128, 4 * 512], BF16))
        wfb = ES3.enter_context(nc.sbuf_tensor("wfb", [128, 4 * 512], BF16))
        bb_b = ES3.enter_context(nc.sbuf_tensor("bb_b", [128, 512], F32))
        zblk = ES3.enter_context(nc.sbuf_tensor("zblk", [128, 2 * 8192], BF16))
        nTo = ES3.enter_context(nc.sbuf_tensor("nTo", [128, 2 * 8192], BF16))
        xot = ES3.enter_context(nc.sbuf_tensor("xot", [128, 4 * 512], F32))
        bo_row = ES3.enter_context(nc.sbuf_tensor("bo_row", [1, 512], F32))
        tmp1 = ES3.enter_context(nc.sbuf_tensor("tmp1", [128, 512], F32))
        tmp2 = ES3.enter_context(nc.sbuf_tensor("tmp2", [128, 512], F32))
        outb = ES3.enter_context(nc.sbuf_tensor("outb", [128, 4 * 512], F32))

        def gload_ep(dst_ap, src_ap):
            add("gpsimd",
                (lambda g, d=dst_ap, s=src_ap: g.dma_start(d, s)),
                waits=[(c_zst.h, ZST_ALL)], inc=(c_w.h, 16))
            return c_w.inc(16)

        gload_ep(g1s[:, 0:128], g1a_d)
        gload_ep(g1s[:, 128:256], g1b_d)
        gload_ep(g2s[:, 0:64].rearrange("e (c gg) -> e c gg", c=4),
                 g2a_d.transpose([1, 0, 2]))
        gload_ep(g2s[:, 64:128].rearrange("e (c gg) -> e c gg", c=4),
                 g2b_d.transpose([1, 0, 2]))
        gload_ep(ebs[:, :], eb_d)
        gload_ep(ets[:, :], et_d)
        gload_ep(ralls[:, :], rall_d)
        gload_ep(talls[:, :], tall_d)
        gload_ep(wo_s[:, :].rearrange("p (fc j) -> p fc j", fc=NF),
                 wo_d.rearrange("(fc p) j -> p fc j", p=128))
        gload_ep(bo_row[0:1, :], bo_d)
        W_EP = c_w.n
        uvev = 0
        last_mac = 0
        last_mm = 0
        for ab in range(2):
            dst = wfa if ab == 0 else wfb
            for c in range(4):
                def fn(t, ab=ab, c=c):
                    t.matmul(pool[0:32, 0:512], g1s[:, ab * 128 + c * 32:
                                                    ab * 128 + c * 32 + 32],
                             ebs[:, :], start=True, stop=True)
                    return t.matmul(pool[0:16, 512:1024],
                                    g2s[:, ab * 64 + c * 16:ab * 64 + c * 16 + 16],
                                    ets[:, :], start=True, stop=True)
                add("tensor", fn,
                    waits=[(c_sc.h, EXP_ALL), (c_w.h, W_EP),
                           (c_zst.h, ZST_ALL),
                           (c_ve.h, max(zpool_free[0], zpool_free[1], uvev,
                                        last_mac))],
                    inc=(c_mm.h, 1))
                uv_mm = c_mm.inc()

                def fn(v):
                    v.tensor_copy(ucs[:, :], pool[0:32, 0:512])
                    return v.tensor_copy(vcs[:, :], pool[0:16, 512:1024])
                add("vector", fn,
                    waits=[(c_mm.h, max(uv_mm, last_mm))],
                    inc=(c_ve.h, 1))
                uvev = c_ve.inc()

                AEX = [SA[:, 0:512], SA[:, 512:1024], SB[:, 0:512],
                       SB[:, 512:1024]]
                BEX = [pool[:, 0:512], pool[:, 512:1024], SC[:, 0:512],
                       SC[:, 512:1024]]

                def fn(t, AEX=AEX, BEX=BEX):
                    last = None
                    for m_ in range(4):
                        t.matmul(AEX[m_], ralls[:, m_ * 128:m_ * 128 + 128],
                                 ucs[:, :], start=True, stop=True)
                        last = t.matmul(BEX[m_],
                                        talls[:, m_ * 128:m_ * 128 + 128],
                                        vcs[:, :], start=True, stop=True)
                    return last
                add("tensor", fn,
                    waits=[(c_ve.h, max(uvev, last_mac))], inc=(c_mm.h, 1))
                last_mm = c_mm.inc()

                def fn(v, ab=ab, c=c, dst=dst, AEX=AEX, BEX=BEX):
                    for half in range(2):
                        m0, m1 = 2 * half, 2 * half + 1
                        v.tensor_copy(tmp1[:, :], BEX[m0])
                        v.tensor_copy(tmp2[:, :], BEX[m1])
                        if c == 0:
                            v.scalar_tensor_tensor(
                                wacc[:, m0 * 512:(m0 + 1) * 512], AEX[m0],
                                1.0, tmp1[:, :], ALU.mult, ALU.mult)
                            v.scalar_tensor_tensor(
                                wacc[:, m1 * 512:(m1 + 1) * 512], AEX[m1],
                                1.0, tmp2[:, :], ALU.mult, ALU.mult)
                        else:
                            v.scalar_tensor_tensor(wtmp[:, :], AEX[m0], 1.0,
                                                   tmp1[:, :], ALU.mult,
                                                   ALU.mult)
                            v.scalar_tensor_tensor(wtmp2[:, :], AEX[m1], 1.0,
                                                   tmp2[:, :], ALU.mult,
                                                   ALU.mult)
                            v.tensor_add(wacc[:, m0 * 512:(m0 + 1) * 512],
                                         wacc[:, m0 * 512:(m0 + 1) * 512],
                                         wtmp[:, :])
                            v.tensor_add(wacc[:, m1 * 512:(m1 + 1) * 512],
                                         wacc[:, m1 * 512:(m1 + 1) * 512],
                                         wtmp2[:, :])
                    last = None
                    if c == 3:
                        for m_ in range(4):
                            if ab == 0:
                                last = v.tensor_copy(
                                    dst[:, m_ * 512:(m_ + 1) * 512],
                                    wacc[:, m_ * 512:(m_ + 1) * 512])
                            else:
                                last = v.tensor_scalar_mul(
                                    dst[:, m_ * 512:(m_ + 1) * 512],
                                    wacc[:, m_ * 512:(m_ + 1) * 512],
                                    sgn_s[:, :])
                    else:
                        last = v.tensor_copy(dumc[:, :], tmp1[:, 0:1])
                    return last
                add("vector", fn, waits=[(c_mm.h, last_mm)], inc=(c_ve.h, 1))
                last_mac = c_ve.inc()
        UNFOLD_VE = last_mac

        add("tensor",
            (lambda t: t.matmul(pool[:, 0:512], ones_r[0:1, 0:128],
                                fbrow[0:1, :], start=True, stop=True)),
            waits=[(c_ve.h, UNFOLD_VE), (c_zst.h, ZST_ALL)], inc=(c_mm.h, 1))
        bbmm = c_mm.inc()
        add("vector",
            (lambda v: v.tensor_copy(bb_b[:, :], pool[:, 0:512])),
            waits=[(c_mm.h, bbmm)], inc=(c_ve.h, 1))
        bbve = c_ve.inc()

        ep_ve = {}
        out_st = {}
        blk_mm = [0] * 4
        for b in range(4):
            bo_ = (b % 2) * 8192
            bp = b % 2
            for zc in range(8):
                add("sync",
                    (lambda e, b=b, zc=zc, bo_=bo_: e.dma_start(
                        zblk[:, bo_ + zc * 1024:bo_ + (zc + 1) * 1024],
                        cc_out[zc * 128:(zc + 1) * 128,
                               b * 1024:(b + 1) * 1024])),
                    waits=[(c_cc.h, CC_ALL),
                           (c_mm.h, blk_mm[b - 2] if b >= 2 else 0)],
                    inc=(c_zb[bp].h, 16))
                c_zb[bp].inc(16)
            for fc in range(NF):
                add("scalar",
                    (lambda e, b=b, fc=fc, bo_=bo_: e.dma_start(
                        nTo[:, bo_ + fc * 1024:bo_ + (fc + 1) * 1024],
                        ntmT_d[fc * 128:(fc + 1) * 128,
                               b * 1024:(b + 1) * 1024])),
                    waits=[(c_w.h, NTMT_ALL), (c_zst.h, ZST_ALL),
                           (c_mm.h, blk_mm[b - 2] if b >= 2 else 0)],
                    inc=(c_nt[bp].h, 16))
                c_nt[bp].inc(16)
            blk_ld = [(c_zb[bp].h, c_zb[bp].n), (c_nt[bp].h, c_nt[bp].n)]

            for ti in range(8):
                gt = b * 8 + ti
                q4 = gt % 4
                xbuf = q4 * 512
                cxo = c_xo[gt % 2]
                add("sync",
                    (lambda e, gt=gt, xbuf=xbuf: e.dma_start(
                        xot[:, xbuf:xbuf + 512],
                        xo_d[gt * 128:(gt + 1) * 128, :])),
                    waits=[(c_ve.h, ep_ve.get(gt - 4, 0)),
                           (c_zst.h, ZST_ALL)],
                    inc=(cxo.h, 16))
                xo_ld = (cxo.h, cxo.inc(16))

                AP_ = [pool, pool, SC, SC][q4]
                ao = [0, 512, 0, 512][q4]
                FP = [SA, SA, SB, SB][q4]
                fo = [0, 512, 0, 512][q4]

                def fn(t, ti=ti, bo_=bo_, AP_=AP_, ao=ao):
                    for zc in range(8):
                        t.matmul(AP_[:, ao:ao + 512],
                                 zblk[:, bo_ + zc * 1024 + ti * 128:
                                      bo_ + zc * 1024 + ti * 128 + 128],
                                 wo_s[:, zc * 512:(zc + 1) * 512],
                                 start=(zc == 0), stop=False)
                    return t.matmul(AP_[:, ao:ao + 512],
                                    ones_r[0:1, 0:128], bo_row[0:1, :],
                                    start=False, stop=True)
                add("tensor", fn,
                    waits=blk_ld + [(c_ve.h, bbve),
                                    (c_ve.h, ep_ve.get(gt - 4, 0)),
                                    (c_ve.h, UNFOLD_VE), (c_w.h, W_EP)],
                    inc=(c_mm.h, 1))
                mm_attn = c_mm.inc()

                def fn(t, ti=ti, bo_=bo_, FP=FP, fo=fo):
                    last = None
                    for fc in range(NF):
                        rhs = (wfa[:, fc * 512:(fc + 1) * 512] if fc < 4 else
                               wfb[:, (fc - 4) * 512:(fc - 3) * 512])
                        last = t.matmul(FP[:, fo:fo + 512],
                                        nTo[:, bo_ + fc * 1024 + ti * 128:
                                            bo_ + fc * 1024 + ti * 128 + 128],
                                        rhs, start=(fc == 0),
                                        stop=(fc == NF - 1))
                    return last
                add("tensor", fn,
                    waits=blk_ld + [(c_ve.h, ep_ve.get(gt - 4, 0)),
                                    (c_ve.h, UNFOLD_VE)],
                    inc=(c_mm.h, 1))
                mm_ffn = c_mm.inc()
                blk_mm[b] = mm_ffn

                def fn(v, gt=gt, ti=ti, xbuf=xbuf, AP_=AP_, ao=ao, FP=FP,
                       fo=fo, q4=q4):
                    v.scalar_tensor_tensor(tmp1[:, :], AP_[:, ao:ao + 512],
                                           1.0, xot[:, xbuf:xbuf + 512],
                                           ALU.mult, ALU.add)
                    v.scalar_tensor_tensor(tmp2[:, :], FP[:, fo:fo + 512],
                                           s_inv[:, gt:gt + 1], bb_b[:, :],
                                           ALU.mult, ALU.add)
                    v.tensor_copy(dumc[:, :], bb_b[:, 0:1])
                    return v.tensor_add(
                        outb[:, q4 * 512:q4 * 512 + 512],
                        tmp1[:, :], tmp2[:, :])
                add("vector", fn,
                    waits=[(c_mm.h, mm_ffn), xo_ld,
                           (c_o[gt % 2].h,
                            out_st.get(gt - 4, (None, 0))[1]
                            if gt >= 4 else 0)],
                    inc=(c_ve.h, 1))
                ep_ve[gt] = c_ve.inc()

                co = c_o[gt % 2]
                add("sync",
                    (lambda e, gt=gt, q4=q4: e.dma_start(
                        out_d[gt * 128:(gt + 1) * 128, :],
                        outb[:, q4 * 512:q4 * 512 + 512])),
                    waits=[(c_ve.h, ep_ve[gt])], inc=(co.h, 16))
                out_st[gt] = (co.h, co.inc(16))

        OUT_ALL = (c_o[0].n, c_o[1].n)

        if dbg:
            def fn(g):
                g.dma_start(dz_d, cc_out[:, :]).then_inc(c_w.h, 16)
                g.dma_start(dn_d, ntm_d[:, :]).then_inc(c_w.h, 16)
                g.dma_start(dnt_d, ntmT_d[:, :]).then_inc(c_w.h, 16)
                g.dma_start(dwf_d[:, 0:2048], wfa[:, :]).then_inc(c_w.h, 16)
                return g.dma_start(dwf_d[:, 2048:4096], wfb[:, :])
            add("gpsimd", fn,
                waits=[(c_o[0].h, OUT_ALL[0]), (c_o[1].h, OUT_ALL[1])],
                inc=(c_w.h, 16))
            c_w.inc(16 * 5)
            add("gpsimd", (lambda g: None), waits=[(c_w.h, c_w.n)])

        add("sync", (lambda e: None),
            waits=[(c_o[0].h, OUT_ALL[0]), (c_o[1].h, OUT_ALL[1])])

        def run_stream(eng_obj, names):
            for ename, waits, fn, inc in OPS:
                if ename in names:
                    for (h, v) in waits:
                        eng_obj.wait_ge(h, v)
                    inst = fn(eng_obj)
                    if inc is not None:
                        assert inst is not None, f"op on {ename} returned None with inc"
                        inst.then_inc(inc[0], inc[1])

        @block.sync
        def _(sy):
            run_stream(sy, ("sync",))

        @block.scalar
        def _(s):
            run_stream(s, ("scalar",))

        @block.vector
        def _(v):
            run_stream(v, ("vector",))

        @block.tensor
        def _(t):
            run_stream(t, ("tensor",))

        @block.gpsimd
        def _(g):
            run_stream(g, ("gpsimd",))

    return nc



def _wo_perm():
    idx = []
    for zc in range(8):
        p, r = zc // 2, zc % 2
        for j in range(2):
            h = 8 * r + 2 * p + j
            idx.extend(range(h * 64, h * 64 + 64))
    return np.array(idx)


def make_consts():
    import ml_dtypes
    mdiag = np.where(np.arange(128)[None, :] >= np.arange(128)[:, None],
                     0.0, -1e30).astype(np.float32)
    mdiagb = mdiag.astype(ml_dtypes.bfloat16)
    ident = np.eye(128, dtype=np.float32).astype(ml_dtypes.bfloat16)
    eb = np.zeros((32, 512), np.float32)
    eb[np.arange(512) // 16, np.arange(512)] = 1.0
    et = np.zeros((16, 512), np.float32)
    et[np.arange(512) % 16, np.arange(512)] = 1.0
    rall = np.zeros((32, 512), np.float32)
    rall[np.arange(512) // 16, np.arange(512)] = 1.0
    tall = np.zeros((16, 512), np.float32)
    tall[np.arange(512) % 16, np.arange(512)] = 1.0
    return mdiag, mdiagb, ident, eb, et, rall, tall


def prep_core(core, L0_attn_x, W_qkv, b_qkv, W_o, b_o,
              G1_R, G2_R, G1_I, G2_I, br, bi):
    b = core // 2
    half = core % 2
    h0 = 8 * half
    o0 = 512 * half
    mdiag, mdiagb, ident, eb, et, rall, tall = make_consts()
    Wr = W_qkv.reshape(16, 64, 3, 1024)
    br_ = b_qkv.reshape(16, 64, 3)
    A = np.ascontiguousarray

    def row(v):
        return A(v.reshape(1, -1).astype(np.float32))

    g1a, g2a = (G1_R, G2_R) if half == 0 else (G1_I, G2_I)
    g1b, g2b = (G1_I, G2_I) if half == 0 else (G1_R, G2_R)
    sgn = np.full((128, 1), -1.0 if half == 0 else 1.0, np.float32)
    fb = br if half == 0 else bi
    return {
        "x": A(L0_attn_x[b]),
        "xo": A(L0_attn_x[b][:, o0:o0 + 512]),
        "wqT": A(Wr[h0:h0 + 8, :, 0, :].reshape(512, 1024).T),
        "wkT": A(Wr[h0:h0 + 8, :, 1, :].reshape(512, 1024).T),
        "wvT": A(Wr[h0:h0 + 8, :, 2, :].reshape(512, 1024).T),
        "woT": A(W_o[o0:o0 + 512].T[_wo_perm()]),
        "bq": row(br_[h0:h0 + 8, :, 0]),
        "bk": row(br_[h0:h0 + 8, :, 1]),
        "bv": row(br_[h0:h0 + 8, :, 2]),
        "bo": row(b_o[o0:o0 + 512]),
        "g1a": A(g1a.reshape(32, 128)),
        "g2a": A(g2a.reshape(4, 16, 16)),
        "g1b": A(g1b.reshape(32, 128)),
        "g2b": A(g2b.reshape(4, 16, 16)),
        "fb": row(fb),
        "sgn": sgn,
        "maskdiag": mdiag,
        "maskdiagb": mdiagb,
        "ident": ident,
        "eb": eb, "et": et, "rall": rall, "tall": tall,
    }


def assemble(results):
    out = np.empty((4, S, D), np.float32)
    for c in range(8):
        b, half = c // 2, c % 2
        out[b][:, half * 512:(half + 1) * 512] = results[c]["out"]
    return out


_NC_CACHE = {}


def kernel(L0_attn_x, L0_attn_W_qkv, L0_attn_b_qkv, L0_attn_W_o, L0_attn_b_o,
           L0_ffn_G1_R, L0_ffn_G2_R, L0_ffn_G1_I, L0_ffn_G2_I,
           L0_ffn_bias_real, L0_ffn_bias_imag):
    x = np.asarray(L0_attn_x, dtype=np.float32)
    args = (x, np.asarray(L0_attn_W_qkv, np.float32),
            np.asarray(L0_attn_b_qkv, np.float32),
            np.asarray(L0_attn_W_o, np.float32),
            np.asarray(L0_attn_b_o, np.float32),
            np.asarray(L0_ffn_G1_R, np.float32),
            np.asarray(L0_ffn_G2_R, np.float32),
            np.asarray(L0_ffn_G1_I, np.float32),
            np.asarray(L0_ffn_G2_I, np.float32),
            np.asarray(L0_ffn_bias_real, np.float32),
            np.asarray(L0_ffn_bias_imag, np.float32))
    if "nc" not in _NC_CACHE:
        _NC_CACHE["nc"] = build()
    nc = _NC_CACHE["nc"]
    in_maps = [prep_core(c, *args) for c in range(8)]
    res = run_bass_kernel_spmd(nc, in_maps, core_ids=list(range(8)))
    return assemble(res.results)


# version 06 (final)
# speedup vs baseline: 1.0402x; 1.0402x over previous
from concourse.bass_utils import run_bass_kernel_spmd
import numpy as np
import concourse.bass as bass
import concourse.mybir as mybir

F32 = mybir.dt.float32
BF16 = mybir.dt.bfloat16
AF = mybir.ActivationFunctionType
ALU = mybir.AluOpType

S = 4096
D = 1024
NT = 32
NF = 8
NPAIR = 4
NSLOT = 4
GRP = 3
SPAN = 1536
EPS_LN = 1e-5
EPS_RMS = 1.1920929e-07
NGRP = (NT + GRP - 1) // GRP


class Ctr:
    def __init__(self, h):
        self.h = h
        self.n = 0

    def inc(self, k=1):
        self.n += k
        return self.n


def build(dbg=False):
    nc = bass.Bass("TRN2", target_bir_lowering=False)

    def din(name, shape, dt=F32):
        return nc.dram_tensor(name, shape, dt, kind="ExternalInput").ap()

    x_d = din("x", [S, D])
    xo_d = din("xo", [S, 512])
    wq_d = din("wqT", [D, 512])
    wk_d = din("wkT", [D, 512])
    wv_d = din("wvT", [D, 512])
    wo_d = din("woT", [D, 512])
    bq_d = din("bq", [1, 512])
    bk_d = din("bk", [1, 512])
    bv_d = din("bv", [1, 512])
    bo_d = din("bo", [1, 512])
    g1a_d = din("g1a", [32, 128])
    g2a_d = din("g2a", [4, 16, 16])
    g1b_d = din("g1b", [32, 128])
    g2b_d = din("g2b", [4, 16, 16])
    fb_d = din("fb", [1, 512])
    sgn_d = din("sgn", [128, 1])
    mdiag_d = din("maskdiag", [128, 128])
    mdiagb_d = din("maskdiagb", [128, 128], BF16)
    ident_d = din("ident", [128, 128], BF16)
    eb_d = din("eb", [32, 512])
    et_d = din("et", [16, 512])
    rall_d = din("rall", [32, 512])
    tall_d = din("tall", [16, 512])

    out_d = nc.dram_tensor("out", [S, 512], F32, kind="ExternalOutput").ap()

    ntm_d = nc.dram_tensor("ntm_scr", [S, D], BF16)
    cc_in = nc.dram_tensor("cc_in", [512, S], BF16)
    cc_out = nc.dram_tensor("cc_out", [1024, S], BF16)
    ntmT_d = nc.dram_tensor("ntmT_scr", [1024, S], BF16)

    dz_d = dn_d = None
    if dbg:
        dz_d = nc.dram_tensor("d_z", [1024, S], BF16, kind="ExternalOutput").ap()
        dn_d = nc.dram_tensor("d_ntm", [S, D], BF16, kind="ExternalOutput").ap()
        dnt_d = nc.dram_tensor("d_ntmT", [1024, S], BF16,
                               kind="ExternalOutput").ap()
        dwf_d = nc.dram_tensor("d_wf", [128, 4096], BF16,
                               kind="ExternalOutput").ap()

    OPS = []

    def add(eng, fn, waits=(), inc=None):
        OPS.append((eng, [(h, v) for (h, v) in waits if v > 0], fn, inc))

    import contextlib
    with contextlib.ExitStack() as ES:
        s_inv = ES.enter_context(nc.sbuf_tensor("s_inv", [128, NT], F32))
        lnt = ES.enter_context(nc.sbuf_tensor("lnt", [128, 32], F32))
        rows = ES.enter_context(nc.sbuf_tensor("rows", [1, 1536], F32))
        fbrow = ES.enter_context(nc.sbuf_tensor("fbrow", [1, 512], F32))
        ones_r = ES.enter_context(nc.sbuf_tensor("ones_r", [1, 512], F32))
        sgn_s = ES.enter_context(nc.sbuf_tensor("sgn_s", [128, 1], F32))
        epsln = ES.enter_context(nc.sbuf_tensor("epsln", [128, 1], F32))
        epsrm = ES.enter_context(nc.sbuf_tensor("epsrm", [128, 1], F32))
        e1e2 = ES.enter_context(nc.sbuf_tensor("e1e2", [128, 1], F32))
        dumc = ES.enter_context(nc.sbuf_tensor("dumc", [128, 1], F32))
        SA = ES.enter_context(nc.psum_tensor("SA", [128, 1024], F32))
        SB = ES.enter_context(nc.psum_tensor("SB", [128, 1024], F32))
        SC = ES.enter_context(nc.psum_tensor("SC", [128, 1024], F32))
        pool = ES.enter_context(nc.psum_tensor("pool", [128, 1024], F32))
        _m_x = ES.enter_context(nc.semaphore("m_x"))
        _m_xb = ES.enter_context(nc.semaphore("m_xb"))
        _m_w = ES.enter_context(nc.semaphore("m_w"))
        _m_nst = ES.enter_context(nc.semaphore("m_nst"))
        _m_tr = ES.enter_context(nc.semaphore("m_tr"))
        _m_zst = ES.enter_context(nc.semaphore("m_zst"))
        _m_cl = ES.enter_context(nc.semaphore("m_cl"))
        _m_out = ES.enter_context(nc.semaphore("m_out"))
        _m_mm = ES.enter_context(nc.semaphore("m_mm"))
        _m_sc = ES.enter_context(nc.semaphore("m_sc"))
        _m_ve = ES.enter_context(nc.semaphore("m_ve"))
        _m_cc = ES.enter_context(nc.semaphore("m_cc"))
        _m_hz = ES.enter_context(nc.semaphore("m_hz"))
        _m_w0 = ES.enter_context(nc.semaphore("m_w0"))
        _m_zb0 = ES.enter_context(nc.semaphore("m_zb0"))
        _m_zb1 = ES.enter_context(nc.semaphore("m_zb1"))
        _m_nt0 = ES.enter_context(nc.semaphore("m_nt0"))
        _m_nt1 = ES.enter_context(nc.semaphore("m_nt1"))
        _m_xo0 = ES.enter_context(nc.semaphore("m_xo0"))
        _m_xo1 = ES.enter_context(nc.semaphore("m_xo1"))
        _m_o0 = ES.enter_context(nc.semaphore("m_o0"))
        _m_o1 = ES.enter_context(nc.semaphore("m_o1"))
        _m_ns0 = ES.enter_context(nc.semaphore("m_ns0"))
        _m_ns1 = ES.enter_context(nc.semaphore("m_ns1"))
        block = ES.enter_context(nc.Block())
        ESQ = ES.enter_context(contextlib.ExitStack())
        Vt = ESQ.enter_context(nc.sbuf_tensor("V_tm", [128, NT * 512], BF16))
        Qa = ESQ.enter_context(nc.sbuf_tensor("Q_all", [128, NPAIR * S], BF16))
        Ka = ESQ.enter_context(nc.sbuf_tensor("K_all", [128, NPAIR * S], BF16))
        ES1 = ES.enter_context(contextlib.ExitStack())
        nT = ES1.enter_context(nc.sbuf_tensor("normedT", [128, NF * S], BF16))
        wq_s = ES1.enter_context(nc.sbuf_tensor("wq_s", [128, NF * 512], BF16))
        wk_s = ES1.enter_context(nc.sbuf_tensor("wk_s", [128, NF * 512], BF16))
        wv_s = ES1.enter_context(nc.sbuf_tensor("wv_s", [128, NF * 512], BF16))
        xin = ES1.enter_context(nc.sbuf_tensor("xin", [128, 2 * D], F32))
        nrm = ES1.enter_context(nc.sbuf_tensor("nrm", [128, 2 * D], BF16))
        c_x, c_xb = Ctr(_m_x), Ctr(_m_xb)
        c_w, c_nst, c_tr = Ctr(_m_w), Ctr(_m_nst), Ctr(_m_tr)
        c_zst, c_cl, c_out = Ctr(_m_zst), Ctr(_m_cl), Ctr(_m_out)
        c_mm, c_sc, c_ve, c_cc = Ctr(_m_mm), Ctr(_m_sc), Ctr(_m_ve), Ctr(_m_cc)
        c_hz = Ctr(_m_hz)

        c_w0 = Ctr(_m_w0)
        c_zb = [Ctr(_m_zb0), Ctr(_m_zb1)]
        c_nt = [Ctr(_m_nt0), Ctr(_m_nt1)]
        c_xo = [Ctr(_m_xo0), Ctr(_m_xo1)]
        c_o = [Ctr(_m_o0), Ctr(_m_o1)]
        c_ns = [Ctr(_m_ns0), Ctr(_m_ns1)]

        def hz(v, inst):
            inst.then_inc(c_hz.h, 1)
            v.wait_ge(c_hz.h, c_hz.inc())

        def ph0(g):
            g.memset(ones_r[:, :], 1.0)
            g.memset(epsln[:, :], EPS_LN)
            g.memset(epsrm[:, :], EPS_RMS)
            g.memset(e1e2[:, :], 1e-5 * 1.1920929e-07)
            g.memset(dumc[:, :], 0.0)
        add("gpsimd", ph0)

        def gload(dst_ap, src_ap):
            add("gpsimd",
                (lambda g, d=dst_ap, s=src_ap: g.dma_start(d, s)),
                inc=(c_w0.h, 16))
            return c_w0.inc(16)

        gload(sgn_s[:, :], sgn_d)
        gload(rows[0:1, 0:512], bq_d)
        gload(rows[0:1, 512:1024], bk_d)
        gload(rows[0:1, 1024:1536], bv_d)
        gload(fbrow[0:1, :], fb_d)
        for wdst, wsrc in ((wq_s, wq_d), (wk_s, wk_d), (wv_s, wv_d)):
            gload(wdst[:, :].rearrange("p (fc j) -> p fc j", fc=NF),
                  wsrc.rearrange("(fc p) j -> p fc j", p=128))
        W_QKV = c_w0.n
        W_ALL = c_w0.n

        ln_done = [0] * NT
        nst_done = [0] * NT
        E2 = 1.1920929e-07
        ln_sc1 = {}
        def ln_stage_a(i):
            buf = (i % 2) * D
            L = (i % 2) * 16
            eng = "sync" if i % 2 == 0 else "scalar"
            cc = c_x if i % 2 == 0 else c_xb
            add(eng,
                (lambda e, i=i, buf=buf: e.dma_start(
                    xin[:, buf:buf + D], x_d[i * 128:(i + 1) * 128, :])),
                waits=[(c_sc.h, ln_done[i - 2] if i >= 2 else 0)],
                inc=(cc.h, 16))
            xl = cc.inc(16)

            def fn(v, buf=buf, L=L):
                v.tensor_scalar(nrm[:, buf:buf + D], xin[:, buf:buf + D], 1.0,
                                0.0, ALU.mult, ALU.add, accum_out=lnt[:, L + 0:L + 1])
                v.scalar_tensor_tensor(nrm[:, buf:buf + D], xin[:, buf:buf + D],
                                       1.0, xin[:, buf:buf + D], ALU.mult,
                                       ALU.mult, accum_out=lnt[:, L + 1:L + 2])
                v.tensor_scalar_mul(lnt[:, L + 2:L + 3], lnt[:, L + 0:L + 1], 1.0 / D)
                v.tensor_scalar_mul(lnt[:, L + 3:L + 4], lnt[:, L + 1:L + 2], 1.0 / D)
                v.tensor_mul(lnt[:, L + 4:L + 5], lnt[:, L + 2:L + 3], lnt[:, L + 2:L + 3])
                v.tensor_copy(lnt[:, L + 15:L + 16], lnt[:, L + 0:L + 1])
                v.scalar_tensor_tensor(lnt[:, L + 5:L + 6], lnt[:, L + 3:L + 4], 1.0,
                                       lnt[:, L + 4:L + 5], ALU.mult, ALU.subtract)
                v.tensor_copy(lnt[:, L + 15:L + 16], lnt[:, L + 0:L + 1])
                return v.scalar_tensor_tensor(lnt[:, L + 6:L + 7], lnt[:, L + 5:L + 6],
                                              1.0 + E2, e1e2[:, :], ALU.mult,
                                              ALU.add)
            add("vector", fn,
                waits=([(cc.h, xl)] +
                       ([(c_ns[nst_done[i - 2][0]].h, nst_done[i - 2][1])]
                        if i >= 2 else [])),
                inc=(c_ve.h, 1))
            ve1 = c_ve.inc()

            def fn(s, L=L):
                s.activation(lnt[:, L + 7:L + 8], lnt[:, L + 6:L + 7], AF.Sqrt)
                return s.activation(lnt[:, L + 8:L + 9], lnt[:, L + 5:L + 6], AF.Sqrt,
                                    bias=epsln[:, :])
            add("scalar", fn, waits=[(c_ve.h, ve1), (c_w0.h, 16)],
                inc=(c_sc.h, 1))
            ln_sc1[i] = c_sc.inc()


        def ln_stage_b(i):
            buf = (i % 2) * D
            L = (i % 2) * 16
            cc = c_x if i % 2 == 0 else c_xb
            def fn(v, i=i, L=L):
                v.reciprocal(lnt[:, L + 9:L + 10], lnt[:, L + 7:L + 8])
                v.reciprocal(lnt[:, L + 10:L + 11], lnt[:, L + 8:L + 9])
                v.tensor_copy(lnt[:, L + 15:L + 16], lnt[:, L + 0:L + 1])
                v.tensor_mul(s_inv[:, i:i + 1], lnt[:, L + 7:L + 8], lnt[:, L + 10:L + 11])
                return v.scalar_tensor_tensor(lnt[:, L + 11:L + 12], lnt[:, L + 2:L + 3], -1.0,
                                              lnt[:, L + 9:L + 10], ALU.mult, ALU.mult)
            add("vector", fn, waits=[(c_sc.h, ln_sc1[i])], inc=(c_ve.h, 1))
            ve2 = c_ve.inc()

            add("scalar",
                (lambda s, i=i, buf=buf, L=L: s.activation(
                    nrm[:, buf:buf + D], xin[:, buf:buf + D], AF.Identity,
                    bias=lnt[:, L + 11:L + 12], scale=lnt[:, L + 9:L + 10])),
                waits=[(c_ve.h, ve2)], inc=(c_sc.h, 1))
            ln_done[i] = c_sc.inc()

            cns = c_ns[i % 2]
            add("gpsimd",
                (lambda g, i=i, buf=buf: g.dma_start(
                    ntm_d[i * 128:(i + 1) * 128, :], nrm[:, buf:buf + D])),
                waits=[(c_sc.h, ln_done[i])], inc=(cns.h, 16))
            nst_done[i] = (i % 2, cns.inc(16))

        for i in range(NT):
            ln_stage_a(i)
            if i >= 1:
                ln_stage_b(i - 1)
        ln_stage_b(NT - 1)
        NST_ALL = (c_ns[0].n, c_ns[1].n)

        for fc in range(NF):
            add("sync",
                (lambda e, fc=fc: e.dma_start(
                    nT[:, fc * S:(fc + 1) * S],
                    ntm_d[:, fc * 128:(fc + 1) * 128], transpose=True)),
                waits=[(c_ns[0].h, NST_ALL[0]), (c_ns[1].h, NST_ALL[1]),
                       (c_tr.h, c_tr.n)],
                inc=(c_tr.h, 16))
            c_tr.inc(16)
            add("gpsimd",
                (lambda g, fc=fc: g.dma_start(
                    ntmT_d[fc * 128:(fc + 1) * 128, :],
                    nT[:, fc * S:(fc + 1) * S])),
                waits=[(c_tr.h, c_tr.n)], inc=(c_w.h, 16))
            c_w.inc(16)
        TR_ALL = c_tr.n
        NTMT_ALL = c_w.n

        pool_free = [0, 0]
        pslot = [0]
        qk_done = [0] * NPAIR

        def build_fm(dst, dcol, tcol, w_s, brow, pcols):
            half = pslot[0] % 2
            pslot[0] += 1
            ps = pool[:, half * 512:half * 512 + 512]

            def fn(t, ps=ps, w_s=w_s, brow=brow, pcols=pcols, tcol=tcol):
                for fc in range(NF):
                    t.matmul(ps, w_s[:, fc * 512 + pcols:fc * 512 + pcols + 128],
                             nT[:, fc * S + tcol:fc * S + tcol + 512],
                             start=(fc == 0), stop=False)
                return t.matmul(ps, brow, ones_r[0:1, 0:512], start=False,
                                stop=True)
            add("tensor", fn,
                waits=[(c_tr.h, TR_ALL), (c_w0.h, W_QKV),
                       (c_ve.h, pool_free[half])],
                inc=(c_mm.h, 1))
            mmv = c_mm.inc()
            add("vector",
                (lambda v, dst=dst, dcol=dcol, ps=ps: v.tensor_copy(
                    dst[:, dcol:dcol + 512], ps)),
                waits=[(c_mm.h, mmv)], inc=(c_ve.h, 1))
            pool_free[half] = c_ve.inc()

        for p in range(NPAIR):
            for tc in range(8):
                build_fm(Qa, p * S + tc * 512, tc * 512, wq_s,
                         rows[0:1, p * 128:p * 128 + 128], p * 128)
                build_fm(Ka, p * S + tc * 512, tc * 512, wk_s,
                         rows[0:1, 512 + p * 128:512 + p * 128 + 128], p * 128)
            qk_done[p] = c_ve.n

        for tt in range(NT):
            half = pslot[0] % 2
            pslot[0] += 1
            ps = pool[:, half * 512:half * 512 + 512]

            def fn(t, tt=tt, ps=ps):
                for fc in range(NF):
                    t.matmul(ps, nT[:, fc * S + tt * 128:fc * S + tt * 128 + 128],
                             wv_s[:, fc * 512:(fc + 1) * 512],
                             start=(fc == 0), stop=False)
                return t.matmul(ps, ones_r[0:1, 0:128], rows[0:1, 1024:1536],
                                start=False, stop=True)
            add("tensor", fn,
                waits=[(c_tr.h, TR_ALL), (c_w0.h, W_QKV),
                       (c_ve.h, pool_free[half])],
                inc=(c_mm.h, 1))
            mmv = c_mm.inc()
            add("vector",
                (lambda v, tt=tt, ps=ps: v.tensor_copy(
                    Vt[:, tt * 512:(tt + 1) * 512], ps)),
                waits=[(c_mm.h, mmv)], inc=(c_ve.h, 1))
            pool_free[half] = c_ve.inc()

        ES1.close()
        PH3_VE = c_ve.n
        ES2 = ES.enter_context(contextlib.ExitStack())
        Ab = ES2.enter_context(nc.sbuf_tensor("A_buf", [128, NSLOT * 2 * S],
                                              BF16))
        z_sb = ES2.enter_context(nc.sbuf_tensor("z_sb", [128, S], F32))
        z_bf = ES2.enter_context(nc.sbuf_tensor("z_bf", [128, S], BF16))
        mdiag = ES2.enter_context(nc.sbuf_tensor("mdiag", [128, 128], F32))
        mdiagb = ES2.enter_context(nc.sbuf_tensor("mdiagb_s", [128, 128], BF16))
        ident = ES2.enter_context(nc.sbuf_tensor("ident_s", [128, 128], BF16))
        inv9 = ES2.enter_context(nc.sbuf_tensor("inv9", [128, 1], F32))
        bscr = ES2.enter_context(nc.sbuf_tensor("bscr", [128, 512], BF16))
        zacc = ES2.enter_context(nc.sbuf_tensor("zacc", [128, 2], F32))
        zpart = ES2.enter_context(nc.sbuf_tensor("zpart", [128, 16], F32))
        mxb = ES2.enter_context(nc.sbuf_tensor("mxb", [128, 32], F32))
        negC = ES2.enter_context(nc.sbuf_tensor("negC", [128, 6], F32))
        rcp = ES2.enter_context(nc.sbuf_tensor("rcp", [128, 2], F32))
        Vp = ES2.enter_context(nc.sbuf_tensor("Vp", [128, NSLOT * 128], BF16))
        add("gpsimd", (lambda g: None), waits=[(c_w.h, NTMT_ALL)])
        add("gpsimd",
            (lambda g: g.dma_start(mdiag[:, :], mdiag_d)),
            waits=[(c_ve.h, PH3_VE)], inc=(c_w.h, 16))
        c_w.inc(16)
        add("gpsimd", (lambda g: g.dma_start(mdiagb[:, :], mdiagb_d)),
            inc=(c_w.h, 16))
        c_w.inc(16)
        add("gpsimd", (lambda g: g.dma_start(ident[:, :], ident_d)),
            inc=(c_w.h, 16))
        c_w.inc(16)
        add("gpsimd", (lambda g: g.memset(inv9[:, :], 1.0 / 9.0)),
            inc=(c_w.h, 1))
        c_w.inc(1)
        MD_LD = c_w.n
        slot_free = [0] * NSLOT
        SA_free = [0]
        SB_free = [0]
        SC_free = [0, 0]
        SCb_free = [0, 0]
        ln_used = [0]
        zpool_free = [0, 0]
        vp_ev_of = {}
        C_done = {}
        p2_exp_of = {}
        p2_meta = {}
        zst_prev = [0]

        def pieces_of(lo, hi, step):
            out = []
            a = lo
            while a < hi:
                out.append((a, min(a + step, hi)))
                a = min(a + step, hi)
            return out

        def emit_strip_p1(p, st):
            gst = p * NT + st
            g = st // GRP
            k0 = st * 128
            gq0 = g * GRP * 128
            loc0 = k0 - gq0
            egrid = S - gq0
            par = st % 2
            beta = False
            pcs = pieces_of(loc0, egrid, 512)
            rnd = 0
            for pi, (a, b) in enumerate(pcs):
                w = b - a
                for h in range(2):
                    lo, hi = (0, 64) if h == 0 else (64, 128)
                    tp = (0, 0) if h == 0 else (64, 0)
                    half = rnd % 2
                    rnd += 1

                    def fn(t, a=a, w=w, lo=lo, hi=hi, tp=tp, p=p, k0=k0,
                           gq0=gq0, half=half, isdiag=(pi == 0)):
                        qg = gq0 + a
                        sc = SC[:, half * 512:half * 512 + w]
                        mmi = t.matmul(sc,
                                       Ka[lo:hi, p * S + k0:p * S + k0 + 128],
                                       Qa[lo:hi, p * S + qg:p * S + qg + w],
                                       start=True, stop=not isdiag,
                                       tile_position=tp)
                        if isdiag:
                            mmi = t.matmul(SC[:, half * 512:half * 512 + 128],
                                           ident[:, :], mdiagb[:, :],
                                           start=False, stop=True)
                        return mmi
                    add("tensor", fn,
                        waits=[(c_ve.h, max(SC_free[half], qk_done[p])),
                               (c_sc.h, SCb_free[half]),
                               (c_w.h, MD_LD)],
                        inc=(c_mm.h, 1))
                    mm = c_mm.inc()

                    mcol = par * 16 + h * 8 + pi
                    if not beta:
                        def fn(v, mm=mm, w=w, mcol=mcol, half=half):
                            return v.reduce_max(
                                mxb[:, mcol:mcol + 1],
                                SC[:, half * 512:half * 512 + w],
                                axis=mybir.AxisListType.X)
                        add("vector", fn, waits=[(c_mm.h, mm)],
                            inc=(c_ve.h, 1))
                        SC_free[half] = c_ve.inc()
                    else:
                        def fn(s, mm=mm, w=w, mcol=mcol, half=half):
                            return s.activation(
                                bscr[:, 0:w],
                                SC[:, half * 512:half * 512 + w], AF.Exp,
                                scale=inv9[:, :],
                                accum_out=mxb[:, mcol:mcol + 1])
                        add("scalar", fn, waits=[(c_mm.h, mm)],
                            inc=(c_sc.h, 1))
                        SCb_free[half] = c_sc.inc()
            npc = len(pcs)

            if not beta:
                def fn(v, par=par, npc=npc):
                    last = None
                    for h in range(2):
                        base = par * 16 + h * 8
                        last = v.reduce_max(
                            negC[:, par * 2 + h:par * 2 + h + 1],
                            mxb[:, base:base + npc],
                            axis=mybir.AxisListType.X, negate=True)
                    return last
                add("vector", fn,
                    waits=[(c_sc.h, p2_exp_of.get(gst - 2, 0))],
                    inc=(c_ve.h, 1))
                C_done[gst] = c_ve.inc()
            else:
                def fn(v, par=par, npc=npc):
                    ins = None
                    for h in range(2):
                        base = par * 16 + h * 8
                        ins = v.tensor_scalar(bscr[:, 0:npc],
                                              mxb[:, base:base + npc],
                                              1.0, 0.0, ALU.mult, ALU.add,
                                              accum_out=negC[:, 4 + h:5 + h])
                    return ins
                add("vector", fn,
                    waits=[(c_sc.h, p2_exp_of.get(gst - 2, 0)),
                           (c_sc.h, SCb_free[0]), (c_sc.h, SCb_free[1]),
                           (c_sc.h, ln_used[0])],
                    inc=(c_ve.h, 1))
                zve = c_ve.inc()

                def fn(s):
                    s.activation(negC[:, 4:5], negC[:, 4:5], AF.Ln)
                    return s.activation(negC[:, 5:6], negC[:, 5:6], AF.Ln)
                add("scalar", fn, waits=[(c_ve.h, zve)], inc=(c_sc.h, 1))
                lnv = c_sc.inc()
                ln_used[0] = lnv

                def fn(v, par=par):
                    v.tensor_scalar_mul(negC[:, par * 2:par * 2 + 1],
                                        negC[:, 4:5], -9.0)
                    return v.tensor_scalar_mul(
                        negC[:, par * 2 + 1:par * 2 + 2], negC[:, 5:6], -9.0)
                add("vector", fn, waits=[(c_sc.h, lnv)], inc=(c_ve.h, 1))
                C_done[gst] = c_ve.inc()

        def emit_strip_p2(p, st):
            gst = p * NT + st
            g = st // GRP
            slot = st % NSLOT
            k0 = st * 128
            gq0 = g * GRP * 128
            loc0 = k0 - gq0
            egrid = S - gq0
            par = st % 2
            acolA = slot * 2 * S
            acolB = slot * 2 * S + S
            spans = pieces_of(loc0, egrid, 1024)
            zc0 = par * 8
            nspan = len(spans)
            prev2 = vp_ev_of.get(gst - 2, 0)
            for spi, (a, b) in enumerate(spans):
                w = b - a
                isfirst = (spi == 0)
                for h in range(2):
                    Sps = SA if h == 0 else SB
                    lo, hi = (0, 64) if h == 0 else (64, 128)
                    tp = (0, 0) if h == 0 else (64, 0)
                    sfree = SA_free if h == 0 else SB_free

                    def fn(t, Sps=Sps, lo=lo, hi=hi, tp=tp, a=a, w=w, p=p,
                           k0=k0, gq0=gq0, isfirst=isfirst):
                        c0 = 0
                        last = None
                        while c0 < w:
                            c1 = min(c0 + 512, w)
                            qg = gq0 + a + c0
                            diag = (isfirst and c0 == 0)
                            last = t.matmul(
                                Sps[:, c0:c1],
                                Ka[lo:hi, p * S + k0:p * S + k0 + 128],
                                Qa[lo:hi, p * S + qg:p * S + qg + (c1 - c0)],
                                start=True, stop=not diag, tile_position=tp)
                            if diag:
                                last = t.matmul(Sps[:, 0:128], ident[:, :],
                                                mdiagb[:, :], start=False,
                                                stop=True)
                            c0 = c1
                        return last
                    add("tensor", fn, waits=[(c_sc.h, sfree[0])],
                        inc=(c_mm.h, 1))
                    mm = c_mm.inc()

                    acol = acolA if h == 0 else acolB
                    zcol = zc0 + (4 * h) + spi

                    def fn(s, Sps=Sps, w=w, a=a, acol=acol, zcol=zcol, h=h,
                           par=par):
                        return s.activation(
                            Ab[:, acol + a:acol + a + w], Sps[:, 0:w], AF.Exp,
                            bias=negC[:, par * 2 + h:par * 2 + h + 1],
                            accum_out=zpart[:, zcol:zcol + 1])
                    add("scalar", fn,
                        waits=[(c_mm.h, mm), (c_ve.h, prev2),
                               (c_ve.h, C_done[gst])],
                        inc=(c_sc.h, 1))
                    sfree[0] = c_sc.inc()

            p2_exp_of[gst] = max(SA_free[0], SB_free[0])
            p2_meta[gst] = (slot, zc0, nspan)

        def emit_vprime(p, st):
            gst = p * NT + st
            slot, zc0, nspan = p2_meta[gst]

            def fn(v, p=p, st=st, slot=slot, zc0=zc0, nspan=nspan):
                if nspan == 1:
                    v.reciprocal(rcp[:, 0:1], zpart[:, zc0:zc0 + 1])
                    v.reciprocal(rcp[:, 1:2], zpart[:, zc0 + 4:zc0 + 5])
                else:
                    for h in range(2):
                        b = zc0 + 4 * h
                        v.tensor_add(zacc[:, h:h + 1], zpart[:, b:b + 1],
                                     zpart[:, b + 1:b + 2])
                    for k in range(2, nspan):
                        for h in range(2):
                            b = zc0 + 4 * h
                            v.tensor_add(zacc[:, h:h + 1], zacc[:, h:h + 1],
                                         zpart[:, b + k:b + k + 1])
                    v.reciprocal(rcp[:, 0:1], zacc[:, 0:1])
                    v.reciprocal(rcp[:, 1:2], zacc[:, 1:2])
                last = None
                for h in range(2):
                    last = v.tensor_scalar_mul(
                        Vp[:, slot * 128 + h * 64:slot * 128 + h * 64 + 64],
                        Vt[:, st * 512 + (2 * p + h) * 64:
                           st * 512 + (2 * p + h) * 64 + 64],
                        rcp[:, h:h + 1])
                return last
            add("vector", fn,
                waits=[(c_sc.h, p2_exp_of[gst]), (c_mm.h, slot_free[slot])],
                inc=(c_ve.h, 1))
            vp_ev_of[gst] = c_ve.inc()

        def emit_av(p, g):
            strips = list(range(g * GRP, min((g + 1) * GRP, NT)))
            gq0 = g * GRP * 128
            egrid = S - gq0
            nchunk = (egrid + 511) // 512
            vpe = max(vp_ev_of[p * NT + st] for st in strips)
            for j in range(nchunk):
                q0 = j * 512
                w = min(512, egrid - q0)
                half = j % 2

                def fn(t, strips=strips, gq0=gq0, j=j, q0=q0, w=w, half=half):
                    live = [st for st in strips
                            if (j > 0 or (st * 128 - gq0) < q0 + w)]
                    last = None
                    for idx, st in enumerate(live):
                        slot = st % NSLOT
                        loc0 = st * 128 - gq0
                        cs = max(0, loc0 - q0) if j == 0 else 0
                        first = (idx == 0)
                        lastf = (idx == len(live) - 1)
                        t.matmul(
                            pool[0:64, half * 512 + cs:half * 512 + w],
                            Vp[:, slot * 128:slot * 128 + 64],
                            Ab[:, slot * 2 * S + q0 + cs:
                               slot * 2 * S + q0 + w],
                            start=first, stop=lastf, tile_position=(0, 0))
                        last = t.matmul(
                            pool[64:128, half * 512 + cs:half * 512 + w],
                            Vp[:, slot * 128 + 64:slot * 128 + 128],
                            Ab[:, slot * 2 * S + S + q0 + cs:
                               slot * 2 * S + S + q0 + w],
                            start=first, stop=lastf, tile_position=(0, 64))
                    return last
                add("tensor", fn,
                    waits=[(c_ve.h, max(vpe, zpool_free[half]))],
                    inc=(c_mm.h, 1))
                mmv = c_mm.inc()
                for st in strips:
                    slot_free[st % NSLOT] = mmv

                def fn(v, g=g, gq0=gq0, q0=q0, w=w, half=half):
                    zp = pool[:, half * 512:half * 512 + w]
                    dst = z_sb[:, gq0 + q0:gq0 + q0 + w]
                    if g == 0:
                        return v.tensor_copy(dst, zp)
                    return v.tensor_add(dst, dst, zp)
                add("vector", fn, waits=[(c_mm.h, mmv)], inc=(c_ve.h, 1))
                zpool_free[half] = c_ve.inc()

        for p in range(NPAIR):
            emit_strip_p1(p, 0)
            emit_strip_p1(p, 1)
            for st in range(NT):
                emit_strip_p2(p, st)
                if st + 2 < NT:
                    emit_strip_p1(p, st + 2)
                emit_vprime(p, st)
                if st % GRP == GRP - 1 or st == NT - 1:
                    emit_av(p, st // GRP)

            def fn(g):
                return g.tensor_copy(z_bf[:, :], z_sb[:, :])
            add("gpsimd", fn,
                waits=[(c_zst.h, zst_prev[0]), (c_ve.h, zpool_free[0]),
                       (c_ve.h, zpool_free[1])],
                inc=(c_w.h, 1))
            zcast = c_w.inc()

            add("sync",
                (lambda e, p=p: e.dma_start(cc_in[p * 128:(p + 1) * 128, :],
                                            z_bf[:, :])),
                waits=[(c_w.h, zcast)], inc=(c_zst.h, 16))
            zst_prev[0] = c_zst.inc(16)

            def fn(g, p=p):
                return g.collective_compute(
                    "AllGather", ALU.bypass,
                    replica_groups=[[0, 1], [2, 3], [4, 5], [6, 7]],
                    ins=[cc_in[p * 128:(p + 1) * 128, :].opt()],
                    outs=[cc_out[p * 256:(p + 1) * 256, :].opt()],
                )
            add("gpsimd", fn, waits=[(c_zst.h, zst_prev[0])], inc=(c_cc.h, 1))
            c_cc.inc()

        ZST_ALL = c_zst.n
        EXP_ALL = c_sc.n
        CC_ALL = c_cc.n

        ES2.close()
        ESQ.close()
        ES3 = ES.enter_context(contextlib.ExitStack())
        wo_s = ES3.enter_context(nc.sbuf_tensor("wo_s", [128, NF * 512], BF16))
        g1s = ES3.enter_context(nc.sbuf_tensor("g1s", [32, 256], F32))
        g2s = ES3.enter_context(nc.sbuf_tensor("g2s", [16, 128], F32))
        ebs = ES3.enter_context(nc.sbuf_tensor("ebs", [32, 512], F32))
        ets = ES3.enter_context(nc.sbuf_tensor("ets", [16, 512], F32))
        ralls = ES3.enter_context(nc.sbuf_tensor("ralls", [32, 512], F32))
        talls = ES3.enter_context(nc.sbuf_tensor("talls", [16, 512], F32))
        ucs = ES3.enter_context(nc.sbuf_tensor("ucs", [32, 512], F32))
        vcs = ES3.enter_context(nc.sbuf_tensor("vcs", [16, 512], F32))
        wtmp = ES3.enter_context(nc.sbuf_tensor("wtmp", [128, 512], F32))
        wtmp2 = ES3.enter_context(nc.sbuf_tensor("wtmp2", [128, 512], F32))
        bt1 = ES3.enter_context(nc.sbuf_tensor("bt1", [128, 512], BF16))
        bt2 = ES3.enter_context(nc.sbuf_tensor("bt2", [128, 512], BF16))
        wacc = ES3.enter_context(nc.sbuf_tensor("wacc", [128, 2048], F32))
        wfa = ES3.enter_context(nc.sbuf_tensor("wfa", [128, 4 * 512], BF16))
        wfb = ES3.enter_context(nc.sbuf_tensor("wfb", [128, 4 * 512], BF16))
        bb_b = ES3.enter_context(nc.sbuf_tensor("bb_b", [128, 512], F32))
        zblk = ES3.enter_context(nc.sbuf_tensor("zblk", [128, 2 * 8192], BF16))
        nTo = ES3.enter_context(nc.sbuf_tensor("nTo", [128, 2 * 8192], BF16))
        xot = ES3.enter_context(nc.sbuf_tensor("xot", [128, 4 * 512], F32))
        bo_row = ES3.enter_context(nc.sbuf_tensor("bo_row", [1, 512], F32))
        tmp1 = ES3.enter_context(nc.sbuf_tensor("tmp1", [128, 512], F32))
        tmp2 = ES3.enter_context(nc.sbuf_tensor("tmp2", [128, 512], F32))
        outb = ES3.enter_context(nc.sbuf_tensor("outb", [128, 4 * 512], F32))

        def gload_ep(dst_ap, src_ap):
            add("gpsimd",
                (lambda g, d=dst_ap, s=src_ap: g.dma_start(d, s)),
                waits=[(c_zst.h, ZST_ALL)], inc=(c_w.h, 16))
            return c_w.inc(16)

        gload_ep(g1s[:, 0:128], g1a_d)
        gload_ep(g1s[:, 128:256], g1b_d)
        gload_ep(g2s[:, 0:64].rearrange("e (c gg) -> e c gg", c=4),
                 g2a_d.transpose([1, 0, 2]))
        gload_ep(g2s[:, 64:128].rearrange("e (c gg) -> e c gg", c=4),
                 g2b_d.transpose([1, 0, 2]))
        gload_ep(ebs[:, :], eb_d)
        gload_ep(ets[:, :], et_d)
        gload_ep(ralls[:, :], rall_d)
        gload_ep(talls[:, :], tall_d)
        gload_ep(wo_s[:, :].rearrange("p (fc j) -> p fc j", fc=NF),
                 wo_d.rearrange("(fc p) j -> p fc j", p=128))
        gload_ep(bo_row[0:1, :], bo_d)
        W_EP = c_w.n
        uvev = 0
        last_mac = 0
        last_mm = 0
        for ab in range(2):
            dst = wfa if ab == 0 else wfb
            for c in range(4):
                def fn(t, ab=ab, c=c):
                    t.matmul(pool[0:32, 0:512], g1s[:, ab * 128 + c * 32:
                                                    ab * 128 + c * 32 + 32],
                             ebs[:, :], start=True, stop=True)
                    return t.matmul(pool[0:16, 512:1024],
                                    g2s[:, ab * 64 + c * 16:ab * 64 + c * 16 + 16],
                                    ets[:, :], start=True, stop=True)
                add("tensor", fn,
                    waits=[(c_sc.h, EXP_ALL), (c_w.h, W_EP),
                           (c_zst.h, ZST_ALL),
                           (c_ve.h, max(zpool_free[0], zpool_free[1], uvev,
                                        last_mac))],
                    inc=(c_mm.h, 1))
                uv_mm = c_mm.inc()

                def fn(v):
                    v.tensor_copy(ucs[:, :], pool[0:32, 0:512])
                    return v.tensor_copy(vcs[:, :], pool[0:16, 512:1024])
                add("vector", fn,
                    waits=[(c_mm.h, max(uv_mm, last_mm))],
                    inc=(c_ve.h, 1))
                uvev = c_ve.inc()

                AEX = [SA[:, 0:512], SA[:, 512:1024], SB[:, 0:512],
                       SB[:, 512:1024]]
                BEX = [pool[:, 0:512], pool[:, 512:1024], SC[:, 0:512],
                       SC[:, 512:1024]]

                def fn(t, AEX=AEX, BEX=BEX):
                    last = None
                    for m_ in range(4):
                        t.matmul(AEX[m_], ralls[:, m_ * 128:m_ * 128 + 128],
                                 ucs[:, :], start=True, stop=True)
                        last = t.matmul(BEX[m_],
                                        talls[:, m_ * 128:m_ * 128 + 128],
                                        vcs[:, :], start=True, stop=True)
                    return last
                add("tensor", fn,
                    waits=[(c_ve.h, max(uvev, last_mac))], inc=(c_mm.h, 1))
                last_mm = c_mm.inc()

                def fn(v, ab=ab, c=c, dst=dst, AEX=AEX, BEX=BEX):
                    for half in range(2):
                        m0, m1 = 2 * half, 2 * half + 1
                        v.tensor_copy(bt1[:, :], BEX[m0])
                        v.tensor_copy(bt2[:, :], BEX[m1])
                        if c == 0:
                            v.scalar_tensor_tensor(
                                wacc[:, m0 * 512:(m0 + 1) * 512], AEX[m0],
                                1.0, bt1[:, :], ALU.mult, ALU.mult)
                            v.scalar_tensor_tensor(
                                wacc[:, m1 * 512:(m1 + 1) * 512], AEX[m1],
                                1.0, bt2[:, :], ALU.mult, ALU.mult)
                        else:
                            v.scalar_tensor_tensor(wtmp[:, :], AEX[m0], 1.0,
                                                   bt1[:, :], ALU.mult,
                                                   ALU.mult)
                            v.scalar_tensor_tensor(wtmp2[:, :], AEX[m1], 1.0,
                                                   bt2[:, :], ALU.mult,
                                                   ALU.mult)
                            v.tensor_add(wacc[:, m0 * 512:(m0 + 1) * 512],
                                         wacc[:, m0 * 512:(m0 + 1) * 512],
                                         wtmp[:, :])
                            v.tensor_add(wacc[:, m1 * 512:(m1 + 1) * 512],
                                         wacc[:, m1 * 512:(m1 + 1) * 512],
                                         wtmp2[:, :])
                    last = None
                    if c == 3:
                        for m_ in range(4):
                            if ab == 0:
                                last = v.tensor_copy(
                                    dst[:, m_ * 512:(m_ + 1) * 512],
                                    wacc[:, m_ * 512:(m_ + 1) * 512])
                            else:
                                last = v.tensor_scalar_mul(
                                    dst[:, m_ * 512:(m_ + 1) * 512],
                                    wacc[:, m_ * 512:(m_ + 1) * 512],
                                    sgn_s[:, :])
                    else:
                        last = v.tensor_copy(dumc[:, :], tmp1[:, 0:1])
                    return last
                add("vector", fn, waits=[(c_mm.h, last_mm)], inc=(c_ve.h, 1))
                last_mac = c_ve.inc()
        UNFOLD_VE = last_mac

        add("tensor",
            (lambda t: t.matmul(pool[:, 0:512], ones_r[0:1, 0:128],
                                fbrow[0:1, :], start=True, stop=True)),
            waits=[(c_ve.h, UNFOLD_VE), (c_zst.h, ZST_ALL)], inc=(c_mm.h, 1))
        bbmm = c_mm.inc()
        add("vector",
            (lambda v: v.tensor_copy(bb_b[:, :], pool[:, 0:512])),
            waits=[(c_mm.h, bbmm)], inc=(c_ve.h, 1))
        bbve = c_ve.inc()

        ep_ve = {}
        out_st = {}
        blk_mm = [0] * 4
        for b in range(4):
            bo_ = (b % 2) * 8192
            bp = b % 2
            for zc in range(8):
                add("sync",
                    (lambda e, b=b, zc=zc, bo_=bo_: e.dma_start(
                        zblk[:, bo_ + zc * 1024:bo_ + (zc + 1) * 1024],
                        cc_out[zc * 128:(zc + 1) * 128,
                               b * 1024:(b + 1) * 1024])),
                    waits=[(c_cc.h, CC_ALL),
                           (c_mm.h, blk_mm[b - 2] if b >= 2 else 0)],
                    inc=(c_zb[bp].h, 16))
                c_zb[bp].inc(16)
            for fc in range(NF):
                add("scalar",
                    (lambda e, b=b, fc=fc, bo_=bo_: e.dma_start(
                        nTo[:, bo_ + fc * 1024:bo_ + (fc + 1) * 1024],
                        ntmT_d[fc * 128:(fc + 1) * 128,
                               b * 1024:(b + 1) * 1024])),
                    waits=[(c_w.h, NTMT_ALL), (c_zst.h, ZST_ALL),
                           (c_mm.h, blk_mm[b - 2] if b >= 2 else 0)],
                    inc=(c_nt[bp].h, 16))
                c_nt[bp].inc(16)
            blk_ld = [(c_zb[bp].h, c_zb[bp].n), (c_nt[bp].h, c_nt[bp].n)]

            for ti in range(8):
                gt = b * 8 + ti
                q4 = gt % 4
                xbuf = q4 * 512
                cxo = c_xo[gt % 2]
                add("sync",
                    (lambda e, gt=gt, xbuf=xbuf: e.dma_start(
                        xot[:, xbuf:xbuf + 512],
                        xo_d[gt * 128:(gt + 1) * 128, :])),
                    waits=[(c_ve.h, ep_ve.get(gt - 4, 0)),
                           (c_zst.h, ZST_ALL)],
                    inc=(cxo.h, 16))
                xo_ld = (cxo.h, cxo.inc(16))

                AP_ = [pool, pool, SC, SC][q4]
                ao = [0, 512, 0, 512][q4]
                FP = [SA, SA, SB, SB][q4]
                fo = [0, 512, 0, 512][q4]

                def fn(t, ti=ti, bo_=bo_, AP_=AP_, ao=ao):
                    for zc in range(8):
                        t.matmul(AP_[:, ao:ao + 512],
                                 zblk[:, bo_ + zc * 1024 + ti * 128:
                                      bo_ + zc * 1024 + ti * 128 + 128],
                                 wo_s[:, zc * 512:(zc + 1) * 512],
                                 start=(zc == 0), stop=False)
                    return t.matmul(AP_[:, ao:ao + 512],
                                    ones_r[0:1, 0:128], bo_row[0:1, :],
                                    start=False, stop=True)
                add("tensor", fn,
                    waits=blk_ld + [(c_ve.h, bbve),
                                    (c_ve.h, ep_ve.get(gt - 4, 0)),
                                    (c_ve.h, UNFOLD_VE), (c_w.h, W_EP)],
                    inc=(c_mm.h, 1))
                mm_attn = c_mm.inc()

                def fn(t, ti=ti, bo_=bo_, FP=FP, fo=fo):
                    last = None
                    for fc in range(NF):
                        rhs = (wfa[:, fc * 512:(fc + 1) * 512] if fc < 4 else
                               wfb[:, (fc - 4) * 512:(fc - 3) * 512])
                        last = t.matmul(FP[:, fo:fo + 512],
                                        nTo[:, bo_ + fc * 1024 + ti * 128:
                                            bo_ + fc * 1024 + ti * 128 + 128],
                                        rhs, start=(fc == 0),
                                        stop=(fc == NF - 1))
                    return last
                add("tensor", fn,
                    waits=blk_ld + [(c_ve.h, ep_ve.get(gt - 4, 0)),
                                    (c_ve.h, UNFOLD_VE)],
                    inc=(c_mm.h, 1))
                mm_ffn = c_mm.inc()
                blk_mm[b] = mm_ffn

                def fn(v, gt=gt, ti=ti, xbuf=xbuf, AP_=AP_, ao=ao, FP=FP,
                       fo=fo, q4=q4):
                    v.scalar_tensor_tensor(tmp1[:, :], AP_[:, ao:ao + 512],
                                           1.0, xot[:, xbuf:xbuf + 512],
                                           ALU.mult, ALU.add)
                    v.scalar_tensor_tensor(tmp2[:, :], FP[:, fo:fo + 512],
                                           s_inv[:, gt:gt + 1], bb_b[:, :],
                                           ALU.mult, ALU.add)
                    v.tensor_copy(dumc[:, :], bb_b[:, 0:1])
                    return v.tensor_add(
                        outb[:, q4 * 512:q4 * 512 + 512],
                        tmp1[:, :], tmp2[:, :])
                add("vector", fn,
                    waits=[(c_mm.h, mm_ffn), xo_ld,
                           (c_o[gt % 2].h,
                            out_st.get(gt - 4, (None, 0))[1]
                            if gt >= 4 else 0)],
                    inc=(c_ve.h, 1))
                ep_ve[gt] = c_ve.inc()

                co = c_o[gt % 2]
                add("sync",
                    (lambda e, gt=gt, q4=q4: e.dma_start(
                        out_d[gt * 128:(gt + 1) * 128, :],
                        outb[:, q4 * 512:q4 * 512 + 512])),
                    waits=[(c_ve.h, ep_ve[gt])], inc=(co.h, 16))
                out_st[gt] = (co.h, co.inc(16))

        OUT_ALL = (c_o[0].n, c_o[1].n)

        if dbg:
            def fn(g):
                g.dma_start(dz_d, cc_out[:, :]).then_inc(c_w.h, 16)
                g.dma_start(dn_d, ntm_d[:, :]).then_inc(c_w.h, 16)
                g.dma_start(dnt_d, ntmT_d[:, :]).then_inc(c_w.h, 16)
                g.dma_start(dwf_d[:, 0:2048], wfa[:, :]).then_inc(c_w.h, 16)
                return g.dma_start(dwf_d[:, 2048:4096], wfb[:, :])
            add("gpsimd", fn,
                waits=[(c_o[0].h, OUT_ALL[0]), (c_o[1].h, OUT_ALL[1])],
                inc=(c_w.h, 16))
            c_w.inc(16 * 5)
            add("gpsimd", (lambda g: None), waits=[(c_w.h, c_w.n)])

        add("sync", (lambda e: None),
            waits=[(c_o[0].h, OUT_ALL[0]), (c_o[1].h, OUT_ALL[1])])

        def run_stream(eng_obj, names):
            for ename, waits, fn, inc in OPS:
                if ename in names:
                    for (h, v) in waits:
                        eng_obj.wait_ge(h, v)
                    inst = fn(eng_obj)
                    if inc is not None:
                        assert inst is not None, f"op on {ename} returned None with inc"
                        inst.then_inc(inc[0], inc[1])

        @block.sync
        def _(sy):
            run_stream(sy, ("sync",))

        @block.scalar
        def _(s):
            run_stream(s, ("scalar",))

        @block.vector
        def _(v):
            run_stream(v, ("vector",))

        @block.tensor
        def _(t):
            run_stream(t, ("tensor",))

        @block.gpsimd
        def _(g):
            run_stream(g, ("gpsimd",))

    return nc



def _wo_perm():
    idx = []
    for zc in range(8):
        p, r = zc // 2, zc % 2
        for j in range(2):
            h = 8 * r + 2 * p + j
            idx.extend(range(h * 64, h * 64 + 64))
    return np.array(idx)


def make_consts():
    import ml_dtypes
    mdiag = np.where(np.arange(128)[None, :] >= np.arange(128)[:, None],
                     0.0, -1e30).astype(np.float32)
    mdiagb = mdiag.astype(ml_dtypes.bfloat16)
    ident = np.eye(128, dtype=np.float32).astype(ml_dtypes.bfloat16)
    eb = np.zeros((32, 512), np.float32)
    eb[np.arange(512) // 16, np.arange(512)] = 1.0
    et = np.zeros((16, 512), np.float32)
    et[np.arange(512) % 16, np.arange(512)] = 1.0
    rall = np.zeros((32, 512), np.float32)
    rall[np.arange(512) // 16, np.arange(512)] = 1.0
    tall = np.zeros((16, 512), np.float32)
    tall[np.arange(512) % 16, np.arange(512)] = 1.0
    return mdiag, mdiagb, ident, eb, et, rall, tall


def prep_core(core, L0_attn_x, W_qkv, b_qkv, W_o, b_o,
              G1_R, G2_R, G1_I, G2_I, br, bi):
    b = core // 2
    half = core % 2
    h0 = 8 * half
    o0 = 512 * half
    mdiag, mdiagb, ident, eb, et, rall, tall = make_consts()
    Wr = W_qkv.reshape(16, 64, 3, 1024)
    br_ = b_qkv.reshape(16, 64, 3)
    A = np.ascontiguousarray

    def row(v):
        return A(v.reshape(1, -1).astype(np.float32))

    g1a, g2a = (G1_R, G2_R) if half == 0 else (G1_I, G2_I)
    g1b, g2b = (G1_I, G2_I) if half == 0 else (G1_R, G2_R)
    sgn = np.full((128, 1), -1.0 if half == 0 else 1.0, np.float32)
    fb = br if half == 0 else bi
    return {
        "x": A(L0_attn_x[b]),
        "xo": A(L0_attn_x[b][:, o0:o0 + 512]),
        "wqT": A(Wr[h0:h0 + 8, :, 0, :].reshape(512, 1024).T),
        "wkT": A(Wr[h0:h0 + 8, :, 1, :].reshape(512, 1024).T),
        "wvT": A(Wr[h0:h0 + 8, :, 2, :].reshape(512, 1024).T),
        "woT": A(W_o[o0:o0 + 512].T[_wo_perm()]),
        "bq": row(br_[h0:h0 + 8, :, 0]),
        "bk": row(br_[h0:h0 + 8, :, 1]),
        "bv": row(br_[h0:h0 + 8, :, 2]),
        "bo": row(b_o[o0:o0 + 512]),
        "g1a": A(g1a.reshape(32, 128)),
        "g2a": A(g2a.reshape(4, 16, 16)),
        "g1b": A(g1b.reshape(32, 128)),
        "g2b": A(g2b.reshape(4, 16, 16)),
        "fb": row(fb),
        "sgn": sgn,
        "maskdiag": mdiag,
        "maskdiagb": mdiagb,
        "ident": ident,
        "eb": eb, "et": et, "rall": rall, "tall": tall,
    }


def assemble(results):
    out = np.empty((4, S, D), np.float32)
    for c in range(8):
        b, half = c // 2, c % 2
        out[b][:, half * 512:(half + 1) * 512] = results[c]["out"]
    return out


_NC_CACHE = {}


def kernel(L0_attn_x, L0_attn_W_qkv, L0_attn_b_qkv, L0_attn_W_o, L0_attn_b_o,
           L0_ffn_G1_R, L0_ffn_G2_R, L0_ffn_G1_I, L0_ffn_G2_I,
           L0_ffn_bias_real, L0_ffn_bias_imag):
    x = np.asarray(L0_attn_x, dtype=np.float32)
    args = (x, np.asarray(L0_attn_W_qkv, np.float32),
            np.asarray(L0_attn_b_qkv, np.float32),
            np.asarray(L0_attn_W_o, np.float32),
            np.asarray(L0_attn_b_o, np.float32),
            np.asarray(L0_ffn_G1_R, np.float32),
            np.asarray(L0_ffn_G2_R, np.float32),
            np.asarray(L0_ffn_G1_I, np.float32),
            np.asarray(L0_ffn_G2_I, np.float32),
            np.asarray(L0_ffn_bias_real, np.float32),
            np.asarray(L0_ffn_bias_imag, np.float32))
    if "nc" not in _NC_CACHE:
        _NC_CACHE["nc"] = build()
    nc = _NC_CACHE["nc"]
    in_maps = [prep_core(c, *args) for c in range(8)]
    res = run_bass_kernel_spmd(nc, in_maps, core_ids=list(range(8)))
    return assemble(res.results)
